# version 09
# baseline (speedup 1.0000x reference)
import numpy as np
import ml_dtypes
import concourse.bass as bass
import concourse.mybir as mybir
from concourse.bass_utils import run_bass_kernel_spmd
from contextlib import ExitStack

F32 = mybir.dt.float32
BF16 = mybir.dt.bfloat16
ALU = mybir.AluOpType
ACTF = mybir.ActivationFunctionType
AX = mybir.AxisListType
MAGIC = float(np.float32(1.5 * 2 ** 23))
QMAX = 32767.0
ENGS = ("sync", "vector", "scalar", "tensor", "gpsimd")


class Config:
    def __init__(self, B=4, S=4096, D=1024, NH=16, DI=3072, ncores=8):
        self.B, self.S, self.D, self.NH, self.DI, self.ncores = B, S, D, NH, DI, ncores
        self.HD = D // NH
        assert self.HD == 64 and NH % 2 == 0
        self.T = S // 2
        self.DCH = D // 128
        self.ICH = DI // 128
        self.STILES = S // 128
        self.NQ = self.T // 128
        self.TQG = min(512, self.T)
        assert self.TQG % 128 == 0 and self.T % self.TQG == 0
        self.NJ = self.TQG // 128
        self.NG = self.T // self.TQG
        self.TG = min(512, self.T)
        self.NTG = self.T // self.TG
        self.KTG = min(512, S)
        self.NPAIR = NH // 2
        assert S % 256 == 0 and D % 512 == 0 and DI % 128 == 0


class Prog:

    def __init__(self, sems):
        self.ops = {e: [] for e in ENGS}
        self.cnt = {}
        self.sems = sems
        self.bar_n = 0

    def do(self, eng, fn):
        self.ops[eng].append(["op", fn, None])

    def dve(self, fn, dep=False):
        if dep:
            self.ops["vector"].append(["op", lambda e: e.drain(), None])
        self.ops["vector"].append(["op", fn, None])

    def inc(self, eng, fn, sem, n=1):
        c = self.cnt.get(sem, 0) + n
        self.cnt[sem] = c
        if n == 1 and eng in ("vector", "scalar"):
            self.ops[eng].append(["op", fn, None])
            self.ops[eng].append(["op", lambda e: e.drain(), (sem, n)])
        else:
            self.ops[eng].append(["op", fn, (sem, n)])
        return c

    def mark(self, eng, sem):
        lst = self.ops[eng]
        if eng in ("vector", "scalar", "tensor", "gpsimd"):
            c = self.cnt.get(sem, 0) + 1
            self.cnt[sem] = c
            lst.append(["op", lambda e: e.drain(), (sem, 1)])
            return c
        last_op = None
        for item in reversed(lst):
            if item[0] == "op":
                last_op = item
                break
        if last_op is not None and last_op[2] is None:
            c = self.cnt.get(sem, 0) + 1
            self.cnt[sem] = c
            last_op[2] = (sem, 1)
        else:
            if last_op is not None:
                s2 = last_op[2][0]
                lst.append(["wait", s2, self.cnt.get(s2, 0)])
            c = self.cnt.get(sem, 0) + 1
            self.cnt[sem] = c
            lst.append(["sinc", sem, 1])
        return c

    def wait(self, eng, sem, val):
        if val <= 0:
            return
        self.ops[eng].append(["wait", sem, val])

    def cur(self, sem):
        return self.cnt.get(sem, 0)

    def barrier(self, flush_dma=False):
        if flush_dma:
            for s_ in ("IN", "SP", "WD", "OUT"):
                self.wait("sync", s_, self.cnt.get(s_, 0))
        self.bar_n += 1
        for e in ENGS:
            self.mark(e, "BR")
        for e in ENGS:
            self.wait(e, "BR", 5 * self.bar_n)


class Carver:
    def __init__(self, ap, n):
        self.ap, self.n, self.off = ap, n, 0

    def f32(self, n):
        assert self.off + n <= self.n, f"misc overflow {self.off}+{n}>{self.n}"
        s = self.ap[:, self.off:self.off + n]
        self.off += n
        return s

    def bf16(self, n):
        assert n % 2 == 0
        return self.f32(n // 2).bitcast(BF16)


def build_nc(cfg):
    nc = bass.Bass("TRN2", target_bir_lowering=False, debug=False,
                   detect_race_conditions=False)
    S, D, T, DI = cfg.S, cfg.D, cfg.T, cfg.DI
    NJ, TQG = cfg.NJ, cfg.TQG

    H = {"cfg": cfg}
    H["x_h"] = nc.dram_tensor("x", [S, D], F32, kind="ExternalInput").ap()
    for nm in ("wq", "wk", "wv", "wo"):
        H[nm + "_h"] = nc.dram_tensor(nm, [D, D], F32, kind="ExternalInput").ap()
    H["wup_h"] = nc.dram_tensor("wup", [DI, D], F32, kind="ExternalInput").ap()
    H["wg_h"] = nc.dram_tensor("wgate", [DI, D], F32, kind="ExternalInput").ap()
    H["wd_h"] = nc.dram_tensor("wdown", [D, DI], F32, kind="ExternalInput").ap()
    H["ident_h"] = nc.dram_tensor("ident", [128, 128], BF16, kind="ExternalInput").ap()
    H["ones_h"] = nc.dram_tensor("ones", [1, 128], F32, kind="ExternalInput").ap()
    H["masks_h"] = nc.dram_tensor("masks", [2 * NJ * 128, TQG], BF16,
                                  kind="ExternalInput").ap()
    H["out_h"] = nc.dram_tensor("out", [T, D], F32, kind="ExternalOutput").ap()
    for nm in ("sp_up", "sp_g", "sp_g2", "sp_o"):
        H[nm] = nc.dram_tensor(nm, [DI, T], F32).ap()
    H["ccio"] = [(nc.dram_tensor(f"cc_in{r}", [16], F32).ap(),
                  nc.dram_tensor(f"cc_out{r}", [16], F32).ap()) for r in range(4)]

    es = ExitStack()
    with es:
        sbt = lambda name, shape, dt: es.enter_context(
            nc.sbuf_tensor(name, shape, dt))
        H["A_xn"] = sbt("A_xn", [128, max(cfg.DCH * S, cfg.ICH * D)], BF16)[:]
        H["A_own"] = sbt("A_own", [128, cfg.NQ * D], F32)[:]
        H["A_q"] = sbt("A_q", [128, max(cfg.DCH * T, cfg.NQ * D,
                                        cfg.ICH * cfg.TG)], BF16)[:]
        MISCN = 8064
        H["A_misc"] = sbt("A_misc", [128, MISCN], F32)[:]
        H["scr"] = sbt("scr", [128, D], F32)[:]
        H["ident"] = sbt("identS", [128, 128], BF16)[:]
        H["ones"] = sbt("onesS", [1, 128], F32)[:]
        H["masks"] = sbt("masksS", [128, 2 * NJ * TQG], BF16)[:]
        H["stat"] = sbt("stat", [128, 16], F32)[:]
        H["mxs"] = sbt("mxs", [128, 8], F32)[:]
        H["scb"] = sbt("scb", [128, 12], F32)[:]
        H["row"] = sbt("rowS", [1, 32], F32)[:]
        H["cst"] = sbt("cstS", [128, 4], F32)[:]
        H["psum"] = [es.enter_context(nc.psum_tensor(f"pb{i}", [128, 512], F32))
                     for i in range(8)]
        sems = {}
        for s_ in ("IN", "WD", "SP", "OUT", "PE", "DV", "AC", "GP", "CC", "BR",
                   "R0", "R1", "R2", "R3", "X0", "X1", "W0", "W1"):
            sems[s_] = es.enter_context(nc.semaphore(s_))
        P = Prog(sems)
        H["carve"] = Carver(H["A_misc"], MISCN)

        plan(cfg, P, H)

        block = es.enter_context(nc.Block())

        def runner(ops):
            def body(e):
                for item in ops:
                    kind = item[0]
                    if kind == "op":
                        inst = item[1](e)
                        if item[2] is not None:
                            sem, n = item[2]
                            inst.then_inc(sems[sem], n)
                    elif kind == "wait":
                        e.wait_ge(sems[item[1]], item[2])
                    else:
                        e.sem_inc(sems[item[1]], item[2])
            return body
        block.sync(runner(P.ops["sync"]))
        block.vector(runner(P.ops["vector"]))
        block.scalar(runner(P.ops["scalar"]))
        block.tensor(runner(P.ops["tensor"]))
        block.gpsimd(runner(P.ops["gpsimd"]))
    return nc


def _dma(P, eng, sem, out, in_):
    return P.inc(eng, lambda e, o=out, i=in_: e.dma_start(out=o, in_=i), sem, 16)


def _mm(P, out, lhsT, rhs, start, stop, tp=None, skip=False):
    return P.inc("tensor",
                 lambda e, o=out, l=lhsT, r=rhs, st=start, sp=stop, t=tp, sk=skip:
                 e.matmul(o, l, r, start=st, stop=sp, tile_position=t,
                          skip_group_check=sk),
                 "PE")



def _finish_dummy(cfg, P, H):
    D, NQ = cfg.D, cfg.NQ
    t = P.cur("DV")
    P.wait("sync", "DV", t)
    P.wait("sync", "AC", P.cur("AC"))
    P.wait("sync", "GP", P.cur("GP"))
    tout = 0
    for J in range(1, NQ):
        tout = _dma(P, "sync", "OUT", H["out_h"][J * 128:(J + 1) * 128, :],
                    H["A_own"][:, J * D:(J + 1) * D])
    tout = _dma(P, "sync", "OUT", H["out_h"][0:128, 0:12], H["scb"][:, :])
    tout = _dma(P, "sync", "OUT", H["out_h"][0:128, 16:24], H["mxs"][:, :])
    tout = _dma(P, "sync", "OUT", H["out_h"][0:128, 32:48], H["stat"][:, :])
    tout = _dma(P, "sync", "OUT", H["out_h"][0:1, 64:96], H["row"][:, :])
    P.wait("sync", "OUT", tout)
    P.wait("gpsimd", "OUT", tout)


def plan(cfg, P, H):
    S, D, T, DI = cfg.S, cfg.D, cfg.T, cfg.DI
    DCH, ICH, NJ, TQG = cfg.DCH, cfg.ICH, cfg.NJ, cfg.TQG
    TG, NTG, NQ, STILES = cfg.TG, cfg.NTG, cfg.NQ, cfg.STILES
    NPAIR, KTG, NG = cfg.NPAIR, cfg.KTG, cfg.NG
    carve, psum = H["carve"], H["psum"]
    A_xn, A_own, A_q, scr = H["A_xn"], H["A_own"], H["A_q"], H["scr"]
    stat, mxs, scb, rowb = H["stat"], H["mxs"], H["scb"], H["row"]
    ident, masks, ones = H["ident"], H["masks"], H["ones"]
    eps = 1e-5
    stop = getattr(cfg, "stop_after", None)
    if stop == "MEMS":
        P.do("vector", lambda e: e.memset(A_own[:, :], 7.0))
        P.mark("vector", "DV")
        return _finish_dummy(cfg, P, H)
    if stop == "XDUMP":
        x_stage0 = carve.f32(D)
        tin0 = _dma(P, "sync", "IN", x_stage0, H["x_h"][0:128, :])
        P.wait("vector", "IN", tin0)
        P.do("vector", lambda e: e.tensor_copy(A_own[:, D:2 * D], x_stage0))
        P.mark("vector", "DV")
        carve.off = 0
        return _finish_dummy(cfg, P, H)

    mean, var, rstd = stat[:, 0:1], stat[:, 1:2], stat[:, 2:3]
    t0, t1, nbias, tmpm = stat[:, 3:4], stat[:, 4:5], stat[:, 5:6], stat[:, 6:7]
    mx_ln, mxu, mxg, mx2, mxo = (mxs[:, i:i + 1] for i in range(5))

    cst = H["cst"]
    for i_, v_ in enumerate((-1.0, MAGIC, 1.0 / 7.0, 1e-12)):
        P.do("vector", lambda e, i=i_, v=v_: e.memset(cst[:, i:i + 1], v))
    P.mark("vector", "DV")
    _dma(P, "sync", "IN", ident[:, :], H["ident_h"][:, :])
    t_ones = _dma(P, "sync", "IN", ones[:, :], H["ones_h"][:, :])
    P.wait("tensor", "IN", t_ones)
    for mi in range(2 * NJ):
        tconst = _dma(P, "sync", "IN", masks[:, mi * TQG:(mi + 1) * TQG],
                      H["masks_h"][mi * 128:(mi + 1) * 128, :])

    tp_free = [0, 0]

    def transpose_chunks(src, nch, dst_fn, grouped_dst=None):
        for half in range((nch + 3) // 4):
            bank = 6 + (half % 2)
            P.wait("tensor", "DV", tp_free[half % 2])
            ndc = min(4, nch - half * 4)
            for i in range(ndc):
                c = half * 4 + i
                tpe = _mm(P, psum[bank][:, i * 128:(i + 1) * 128],
                          src[:, c * 128:(c + 1) * 128], ident[:, :],
                          True, True)
            tpe = P.mark("tensor", "PE")
            P.wait("vector", "PE", tpe)
            if grouped_dst is not None:
                dsts = [(grouped_dst(half, ndc),
                         psum[bank][:, 0:ndc * 128].rearrange(
                             "p (c t) -> p c t", t=128))]
            else:
                dsts = [(dst_fn(half * 4 + i),
                         psum[bank][:, i * 128:(i + 1) * 128])
                        for i in range(ndc)]
            for d, s_ in dsts:
                P.do("vector", lambda e, d=d, s=s_: e.tensor_copy(d, s))
            tp_free[half % 2] = P.mark("vector", "DV")

    x_stage = carve.f32(2 * D)
    xnbf = carve.bf16(2 * D)

    ln_max_tkt = 0
    n_own_seen = 0
    last_act = 0
    xs_ac = [0, 0]
    xb_pe = [0, 0]
    for pt in range(STILES):
        par = pt % 2
        is_own = (pt % 2 == 0)
        j = pt // 2
        xs = x_stage[:, par * D:(par + 1) * D]
        P.wait("sync", "AC", xs_ac[par])
        tin = _dma(P, "sync", f"X{par}", xs, H["x_h"][pt * 128:(pt + 1) * 128, :])
        P.wait("vector", f"X{par}", tin)
        P.wait("vector", "AC", last_act)
        P.do("vector", lambda e, xs=xs: e.reduce_sum(mean, xs, AX.X))
        P.dve(lambda e: e.tensor_scalar_mul(mean, mean, 1.0 / D), dep=True)
        P.dve(lambda e, xs=xs: e.scalar_tensor_tensor(
            scr[:, 0:D], xs, mean[:, 0:1], xs, ALU.subtract, ALU.mult), dep=True)
        P.dve(lambda e: e.reduce_sum(var, scr[:, 0:D], AX.X), dep=True)
        P.dve(lambda e: e.tensor_scalar(var, var, 1.0 / D, eps,
                                        ALU.mult, ALU.add), dep=True)
        P.dve(lambda e: e.reciprocal(t0, var), dep=True)
        td = P.mark("vector", "DV")
        P.wait("scalar", "DV", td)
        ts_ = P.inc("scalar", lambda e: e.activation(rstd, t0, ACTF.Sqrt), "AC")
        P.wait("vector", "AC", ts_)
        P.dve(lambda e: e.tensor_tensor(t1, rstd, rstd, ALU.mult), dep=True)
        P.dve(lambda e: e.tensor_tensor(t1, t1, var, ALU.mult), dep=True)
        P.dve(lambda e: e.tensor_scalar(t1, t1, -0.5, 1.5,
                                        ALU.mult, ALU.add), dep=True)
        P.dve(lambda e: e.tensor_tensor(rstd, rstd, t1, ALU.mult), dep=True)
        P.dve(lambda e: e.scalar_tensor_tensor(
            nbias, mean, -1.0, rstd, ALU.mult, ALU.mult), dep=True)
        tdv = P.mark("vector", "DV")
        P.wait("scalar", "DV", tdv)
        P.wait("scalar", "PE", xb_pe[par])
        xb = xnbf[:, par * D:(par + 1) * D]
        if is_own:
            dst = A_own[:, j * D:(j + 1) * D]
            last_act = P.inc("scalar", lambda e, xs=xs, d=dst: e.activation(
                d, xs, ACTF.Identity, bias=nbias[:, 0:1], scale=rstd[:, 0:1]),
                "AC")
            xs_ac[par] = last_act
            P.wait("vector", "AC", last_act)
            P.wait("vector", "PE", xb_pe[par])
            P.do("vector", lambda e, d=dst, xb=xb: e.tensor_copy(xb, d))
            P.do("vector", lambda e, d=dst: e.reduce_max(
                tmpm, d, AX.X, apply_absolute_value=True))
            if n_own_seen == 0:
                P.dve(lambda e: e.tensor_copy(mx_ln, tmpm), dep=True)
            else:
                P.dve(lambda e: e.tensor_tensor(mx_ln, mx_ln, tmpm,
                                                ALU.max), dep=True)
            n_own_seen += 1
            if n_own_seen == NQ:
                ln_max_tkt = P.mark("vector", "DV")
        else:
            last_act = P.inc("scalar", lambda e, xs=xs, xb=xb: e.activation(
                xb, xs, ACTF.Identity, bias=nbias[:, 0:1], scale=rstd[:, 0:1]),
                "AC")
            xs_ac[par] = last_act
        tb = P.mark("vector", "DV")
        if not is_own:
            P.wait("tensor", "AC", last_act)
        P.wait("tensor", "DV", tb)
        transpose_chunks(
            xb, DCH, None,
            grouped_dst=lambda half, ndc, pt=pt: A_xn[:].rearrange(
                "p (c s) -> p c s", s=S)[:, half * 4:half * 4 + ndc,
                                         pt * 128:(pt + 1) * 128])
        xb_pe[par] = P.cur("PE")
        if stop == f"LNT{pt}":
            return _finish_dummy(cfg, P, H)

    if stop == "LN":
        return _finish_dummy(cfg, P, H)
    P.wait("gpsimd", "DV", ln_max_tkt)
    tcc = _collective(P, H, 0, mx_ln)
    P.wait("vector", "GP", tcc)
    if stop == "CC1":
        return _finish_dummy(cfg, P, H)

    def scale_ops(src_i, dst_i):
        a = rowb[0:1, src_i:src_i + 1]
        s_ = rowb[0:1, dst_i:dst_i + 1]
        iv = rowb[0:1, dst_i + 1:dst_i + 2]
        return [
            lambda e, a=a, s_=s_: e.tensor_single_scalar(s_, a, 1.0 / QMAX,
                                                         ALU.mult),
            lambda e: e.drain(),
            lambda e, s_=s_: e.tensor_scalar_max(s_, s_, 1e-12),
            lambda e: e.drain(),
            lambda e, s_=s_, iv=iv: e.reciprocal(iv, s_),
            lambda e: e.drain(),
        ]
    t_scb1 = _bcast_scalars(P, H, scale_ops(0, 0), 2, scb[:, 0:2], tp_free)
    s1b, inv1b = scb[:, 0:1], scb[:, 1:2]
    if stop == "BC1":
        return _finish_dummy(cfg, P, H)

    xqbf = xnbf
    P.wait("vector", "DV", t_scb1)
    for j in range(NQ):
        par = j % 2
        src = A_own[:, j * D:(j + 1) * D]
        qb = xqbf[:, par * D:(par + 1) * D]
        P.wait("vector", "PE", xb_pe[par])
        if True:
            P.dve(lambda e, s=src: e.tensor_scalar(
                scr[:, 0:D], s, inv1b, MAGIC, ALU.mult, ALU.add), dep=True)
            P.dve(lambda e, q=qb: e.tensor_scalar(
                q, scr[:, 0:D], MAGIC, s1b, ALU.subtract, ALU.mult), dep=True)
        tb = P.mark("vector", "DV")
        P.wait("tensor", "DV", tb)
        if getattr(cfg, "xq_no_tp", False) and j >= 2:
            xb_pe[par] = P.cur("PE")
            if stop == f"XQ{j}":
                return _finish_dummy(cfg, P, H)
            continue
        transpose_chunks(
            qb, DCH, None,
            grouped_dst=lambda half, ndc, j=j: A_q[:].rearrange(
                "p (c t) -> p c t", t=T)[:, half * 4:half * 4 + ndc,
                                         j * 128:(j + 1) * 128])
        xb_pe[par] = P.cur("PE")
        if stop == f"XQ{j}":
            return _finish_dummy(cfg, P, H)

    P.barrier()

    if stop == "P0":
        return _finish_dummy(cfg, P, H)
    carve.off = 0
    wupraw = carve.f32(D)
    wgraw = carve.f32(D)
    stg = carve.f32(4 * TG)
    bmxA = carve.f32(D // 2)
    bmxB = carve.f32(max(D // 4, 16))
    bs_s = carve.f32(D // 32)
    bs_i = carve.f32(D // 32)
    wdqu = carve.bf16(D)
    wdqg = carve.bf16(D)
    wTu = carve.bf16(2 * DCH * 128)
    wTg = carve.bf16(2 * DCH * 128)

    def dequant(raw, nblk, wdq, tin, bA, bB, ss, si, insem="IN"):
        W = nblk * 32
        P.wait("vector", insem, tin)
        P.do("vector", lambda e, r=raw, s=scr[:, 0:W]: e.scalar_tensor_tensor(
            s, r, -1.0, r, ALU.mult, ALU.max))
        aw = scr[:, 0:W].rearrange("p (b k) -> p b k", k=32)
        P.dve(lambda e, a=aw, n=nblk: e.tensor_tensor(
            bA[:, 0:n * 16].rearrange("p (b k) -> p b k", k=16),
            a[:, :, 0:16], a[:, :, 16:32], ALU.max), dep=True)
        cur, w, i = bA[:, 0:nblk * 16].rearrange("p (b k) -> p b k", k=16), 16, 0
        while w > 2:
            hw = w // 2
            nxt = [bB, bA][i % 2][:, 0:nblk * hw].rearrange(
                "p (b k) -> p b k", k=hw)
            P.dve(lambda e, n=nxt, c=cur, h=hw, w=w: e.tensor_tensor(
                n, c[:, :, 0:h], c[:, :, h:w], ALU.max), dep=True)
            cur, w, i = nxt, hw, i + 1
        P.dve(lambda e, c=cur, n=nblk, ss=ss: e.tensor_tensor(
            ss[:, 0:n], c[:, :, 0:1].squeeze(2), c[:, :, 1:2].squeeze(2),
            ALU.max), dep=True)
        P.dve(lambda e, n=nblk, ss=ss: e.tensor_scalar(
            ss[:, 0:n], ss[:, 0:n], 1.0 / 7.0, 1e-12, ALU.mult, ALU.max),
            dep=True)
        P.dve(lambda e, n=nblk, ss=ss, si=si: e.reciprocal(
            si[:, 0:n], ss[:, 0:n]), dep=True)
        ib = si[:, 0:nblk].unsqueeze(2).broadcast_to((128, nblk, 32))
        sbb = ss[:, 0:nblk].unsqueeze(2).broadcast_to((128, nblk, 32))
        a = raw.rearrange("p (b k) -> p b k", k=32)
        s3 = scr[:, 0:W].rearrange("p (b k) -> p b k", k=32)
        P.dve(lambda e, a=a, ib=ib, s=s3: e.tensor_tensor(
            s, a, ib, ALU.mult), dep=True)
        P.dve(lambda e, s=scr[:, 0:W]: e.tensor_scalar(
            s, s, MAGIC, MAGIC, ALU.add, ALU.subtract), dep=True)
        P.dve(lambda e, s=s3, sbb=sbb,
             d=wdq.rearrange("p (b k) -> p b k", k=32): e.scalar_tensor_tensor(
            d, s, 1.0, sbb, ALU.mult, ALU.mult), dep=True)
        return P.mark("vector", "DV")

    bank_guard = {}
    stg_dma = {}
    stg_dv = {}
    mx_first = {"u": True, "g": True}
    wT_read = [0, 0]
    raw_free = 0
    for ic in range(ICH):
        par = ic % 2
        P.wait("sync", "DV", raw_free)
        tinU = _dma(P, "sync", "W0", wupraw, H["wup_h"][ic * 128:(ic + 1) * 128, :])
        tinG = _dma(P, "sync", "W1", wgraw, H["wg_h"][ic * 128:(ic + 1) * 128, :])
        tqU = dequant(wupraw, D // 32, wdqu, tinU, bmxA, bmxB, bs_s, bs_i,
                      "W0")
        tqG = dequant(wgraw, D // 32, wdqg, tinG, bmxA, bmxB, bs_s, bs_i,
                      "W1")
        raw_free = tqG
        P.wait("tensor", "DV", tqU)
        P.wait("vector", "PE", wT_read[par])
        transpose_chunks(wdqu, DCH,
                         lambda c, par=par: wTu[:, par * DCH * 128 + c * 128:
                                                par * DCH * 128 + (c + 1) * 128])
        P.wait("tensor", "DV", tqG)
        transpose_chunks(wdqg, DCH,
                         lambda c, par=par: wTg[:, par * DCH * 128 + c * 128:
                                                par * DCH * 128 + (c + 1) * 128])
        tT = P.cur("DV")
        if stop == f"P1T{ic}":
            return _finish_dummy(cfg, P, H)
        P.wait("tensor", "DV", tT)
        for w_i, (wT, bpair, spill, mxc, mk) in enumerate((
                (wTu, (0, 1), H["sp_up"], mxu, "u"),
                (wTg, (2, 3), H["sp_g"], mxg, "g"))):
            for tg_i in range(NTG):
                bank = bpair[tg_i % 2]
                for sem, tk in bank_guard.get(bank, ()):
                    P.wait("tensor", sem, tk)
                for dc in range(DCH):
                    tpe = _mm(P, psum[bank][:, 0:TG],
                              wT[:, par * DCH * 128 + dc * 128:
                                 par * DCH * 128 + (dc + 1) * 128],
                              A_q[:, dc * T + tg_i * TG:dc * T + (tg_i + 1) * TG],
                              dc == 0, dc == DCH - 1)
                slot = stg[:, (w_i * 2 + tg_i % 2) * TG:
                           (w_i * 2 + tg_i % 2 + 1) * TG]
                key = (w_i, tg_i % 2)
                tpe = P.mark("tensor", "PE")
                P.wait("scalar", "PE", tpe)
                if key in stg_dma:
                    P.wait("scalar", "SP", stg_dma[key])
                if key in stg_dv:
                    P.wait("scalar", "DV", stg_dv[key])
                tev = P.inc("scalar", lambda e, d=slot, s=psum[bank][:, 0:TG]:
                            e.activation(d, s, ACTF.Identity), "AC")
                bank_guard[bank] = [("AC", tev)]
                P.wait("vector", "AC", tev)
                P.do("vector", lambda e, s=slot: e.reduce_max(
                    tmpm, s, AX.X, apply_absolute_value=True))
                if mx_first[mk]:
                    P.dve(lambda e, m=mxc: e.tensor_copy(m, tmpm), dep=True)
                    mx_first[mk] = False
                else:
                    P.dve(lambda e, m=mxc: e.tensor_tensor(
                        m, m, tmpm, ALU.max), dep=True)
                stg_dv[key] = P.mark("vector", "DV")
                P.wait("scalar", "AC", tev)
                dst = spill[ic * 128:(ic + 1) * 128, tg_i * TG:(tg_i + 1) * TG]
                stg_dma[key] = _dma(P, "scalar", "SP", dst, slot)
        wT_read[par] = P.cur("PE")
        if stop == f"P1C{ic}":
            return _finish_dummy(cfg, P, H)
    ffn_mx_tkt = P.cur("DV")

    P.barrier()

    if stop == "P1":
        P.wait("sync", "SP", P.cur("SP"))
        slot_i = 0
        for sp in (H["sp_up"], H["sp_g"]):
            for ic_d in (0, 12):
                for tg_d in range(NTG):
                    tpd = _dma(P, "sync", "IN",
                               H["A_own"][:, D + slot_i * TG:D + (slot_i + 1) * TG],
                               sp[ic_d * 128:(ic_d + 1) * 128,
                                  tg_d * TG:(tg_d + 1) * TG])
                    slot_i += 1
        P.wait("sync", "IN", tpd)
        return _finish_dummy(cfg, P, H)
    carve.off = 0
    wqkv = carve.bf16(3 * DCH * 128)
    K_hp = carve.bf16(S)
    Q_hp = carve.bf16(T)
    V_hp = carve.bf16(STILES * 130)
    Pt = carve.bf16(4 * TQG)
    rden = carve.f32(2)

    P.do("vector", lambda e: e.memset(
        V_hp.rearrange("p (t c) -> p t c", c=130)[:, :, 64:130:65], 1.0))
    P.mark("vector", "DV")

    sc_free = {}
    pt_free = {}
    ctx_free = [0, 0]
    kqv_pe = 0
    wqkv_pe = 0
    kqv_dv = 0

    for hp in range(NPAIR):
        par = hp % 2
        base = 0
        P.wait("gpsimd", "PE", wqkv_pe)
        tw = 0
        for w_i, w_h in enumerate((H["wk_h"], H["wq_h"], H["wv_h"])):
            for dc in range(DCH):
                dst = wqkv[:, base + w_i * DCH * 128 + dc * 128:
                           base + w_i * DCH * 128 + (dc + 1) * 128]
                src = w_h[dc * 128:(dc + 1) * 128, hp * 128:(hp + 1) * 128]
                tw = P.inc("gpsimd", lambda e, d=dst, s=src:
                           e.dma_start(out=d, in_=s), "WD", 16)
        kt = wqkv[:, base:base + DCH * 128]
        qt = wqkv[:, base + DCH * 128:base + 2 * DCH * 128]
        vt = wqkv[:, base + 2 * DCH * 128:base + 3 * DCH * 128]
        P.wait("tensor", "WD", tw)
        P.wait("tensor", "PE", kqv_pe)
        P.wait("vector", "PE", kqv_pe)
        pb = [0, 0]
        for kg in range(S // KTG):
            bank = 6 + kg % 2
            P.wait("tensor", "DV", pb[kg % 2])
            for dc in range(DCH):
                tpe = _mm(P, psum[bank][:, 0:KTG],
                          kt[:, dc * 128:(dc + 1) * 128],
                          A_xn[:, dc * S + kg * KTG:dc * S + (kg + 1) * KTG],
                          dc == 0, dc == DCH - 1)
            tpe = P.mark("tensor", "PE")
            P.wait("vector", "PE", tpe)
            P.do("vector", lambda e, d=K_hp[:, kg * KTG:(kg + 1) * KTG],
                 s=psum[bank][:, 0:KTG]: e.tensor_copy(d, s))
            pb[kg % 2] = P.mark("vector", "DV")
        QG = min(KTG, T)
        for qg in range(T // QG):
            bank = 6 + qg % 2
            P.wait("tensor", "DV", pb[qg % 2])
            ntq = QG // 128
            for dc in range(DCH):
                src = A_xn[:, dc * S + qg * 2 * QG:
                           dc * S + (qg + 1) * 2 * QG].rearrange(
                    "p (t q) -> p t q", q=256)[:, :, 0:128]
                tpe = _mm(P, psum[bank][:, 0:QG],
                          qt[:, dc * 128:(dc + 1) * 128], src,
                          dc == 0, dc == DCH - 1)
            tpe = P.mark("tensor", "PE")
            P.wait("vector", "PE", tpe)
            P.do("vector", lambda e, d=Q_hp[:, qg * QG:(qg + 1) * QG],
                 s=psum[bank][:, 0:QG]: e.tensor_copy(d, s))
            pb[qg % 2] = P.mark("vector", "DV")
        for vg in range((STILES + 3) // 4):
            bank = 6 + vg % 2
            P.wait("tensor", "DV", pb[vg % 2])
            nvt = min(4, STILES - vg * 4)
            for m in range(nvt):
                vti = vg * 4 + m
                for dc in range(DCH):
                    tpe = _mm(P, psum[bank][:, m * 128:m * 128 + 128],
                              A_xn[:, dc * S + vti * 128:dc * S + (vti + 1) * 128],
                              vt[:, dc * 128:(dc + 1) * 128],
                              dc == 0, dc == DCH - 1)
            tpe = P.mark("tensor", "PE")
            P.wait("vector", "PE", tpe)
            src = psum[bank][:, 0:nvt * 128].rearrange(
                "p (m h d) -> p m h d", h=2, d=64)
            dst = V_hp[:, vg * 4 * 130:(vg * 4 + nvt) * 130].rearrange(
                "p (m c) -> p m c", c=130).rearrange(
                "p m (h d) -> p m h d", h=2, d=65)[:, :, :, 0:64]
            P.do("vector", lambda e, d=dst, s=src: e.tensor_copy(d, s))
            pb[vg % 2] = P.mark("vector", "DV")
        kqv_dv = P.cur("DV")
        wqkv_pe = P.cur("PE")
        P.wait("tensor", "DV", kqv_dv)

        for G in range(NG):
            KT = 2 * NJ * (G + 1)
            m0 = 2 * NJ * G
            cons = {}

            for k in range(KT + 1):
                kp = k % 2
                if k < KT:
                    for h in range(2):
                        bank = kp * 2 + h
                        if (kp, h) in sc_free:
                            P.wait("tensor", "AC", sc_free[(kp, h)])
                        t_sc = _mm(P, psum[bank][:, 0:TQG],
                                   K_hp[h * 64:(h + 1) * 64,
                                        k * 128:(k + 1) * 128],
                                   Q_hp[h * 64:(h + 1) * 64,
                                        G * TQG:(G + 1) * TQG],
                                   True, True, tp=(h * 64, 0))
                if k >= 1:
                    kk = k - 1
                    kkp = kk % 2
                    sem, tv = cons.pop(kk)
                    P.wait("tensor", sem, tv)
                    if kk == 0:
                        P.wait("tensor", "DV", ctx_free[0])
                        P.wait("tensor", "DV", ctx_free[1])
                    for h in range(2):
                        cb = psum[4 + h]
                        for jj in range(NJ):
                            _mm(P, cb[:, jj * 65:jj * 65 + 65],
                                Pt[:, (kkp * 2 + h) * TQG + jj * 128:
                                   (kkp * 2 + h) * TQG + (jj + 1) * 128],
                                V_hp[:, kk * 130 + h * 65:kk * 130 + (h + 1) * 65],
                                kk == 0 and jj == 0,
                                kk == KT - 1 and jj == NJ - 1, skip=True)
                    pt_free[kkp] = P.cur("PE")
                if k < KT:
                    t_sc = P.mark("tensor", "PE")
                    P.wait("scalar", "PE", t_sc)
                    if kp in pt_free:
                        P.wait("scalar", "PE", pt_free[kp])
                    for h in range(2):
                        bank = kp * 2 + h
                        dst = Pt[:, (kp * 2 + h) * TQG:(kp * 2 + h + 1) * TQG]
                        tac = P.inc("scalar", lambda e, d=dst,
                                    s=psum[bank][:, 0:TQG]: e.activation(
                                    d, s, ACTF.Exp, scale=0.125), "AC")
                        sc_free[(kp, h)] = tac
                    if k >= m0:
                        mi = k - m0
                        P.wait("vector", "AC", tac)
                        if k == m0:
                            P.wait("vector", "IN", tconst)
                        for h in range(2):
                            pa = Pt[:, (kp * 2 + h) * TQG:(kp * 2 + h + 1) * TQG]
                            P.do("vector", lambda e, p=pa,
                                 m=masks[:, mi * TQG:(mi + 1) * TQG]:
                                 e.tensor_tensor(p, p, m, ALU.mult))
                        cons[k] = ("DV", P.mark("vector", "DV"))
                    else:
                        cons[k] = ("AC", tac)
            t_ctx = P.mark("tensor", "PE")
            P.wait("vector", "PE", t_ctx)
            for h in range(2):
                cb = psum[4 + h]
                hg = hp * 2 + h
                for jj in range(NJ):
                    J = G * NJ + jj
                    P.do("vector", lambda e, s=cb[:, jj * 65 + 64:jj * 65 + 65]:
                         e.reciprocal(rden[:, 0:1], s))
                    dst = A_q[:, J * D + hg * 64:J * D + hg * 64 + 64]
                    P.dve(lambda e, d=dst,
                          s=cb[:, jj * 65:jj * 65 + 64]:
                          e.tensor_scalar_mul(d, s, rden[:, 0:1]), dep=True)
                ctx_free[h] = P.mark("vector", "DV")
        kqv_pe = P.cur("PE")

    P.barrier()

    if stop == "P2":
        return _finish_dummy(cfg, P, H)
    carve.off = 0
    ctx_fm = carve.bf16(2 * DCH * 128)
    wo_sb = carve.bf16(DCH * D)
    x_st3 = carve.f32(2 * D)

    two = 0
    for dc in range(DCH):
        two = P.inc("gpsimd", lambda e,
                    d=wo_sb[:, dc * D:(dc + 1) * D],
                    s=H["wo_h"][dc * 128:(dc + 1) * 128, :]:
                    e.dma_start(out=d, in_=s), "WD", 16)
    wo_pe = [0, 0]
    x3_dv = [0, 0]
    res_tkt = {}
    for J in range(NQ):
        par = J % 2
        P.wait("sync", "DV", x3_dv[par])
        tin = _dma(P, "sync", "IN", x_st3[:, par * D:(par + 1) * D],
                   H["x_h"][2 * J * 128:(2 * J + 1) * 128, :])
        P.wait("vector", "PE", wo_pe[par])
        transpose_chunks(
            A_q[:, J * D:(J + 1) * D], DCH,
            lambda c, par=par: ctx_fm[:, par * DCH * 128 + c * 128:
                                      par * DCH * 128 + (c + 1) * 128])
        t_cfm = P.cur("DV")
        P.wait("tensor", "DV", t_cfm)
        if J == 0:
            P.wait("tensor", "WD", two)
        for do2 in range(D // 512):
            bank = par * 2 + do2
            if bank in res_tkt:
                P.wait("tensor", "DV", res_tkt[bank])
            for dc in range(DCH):
                tpe = _mm(P, psum[bank][:, 0:512],
                          ctx_fm[:, par * DCH * 128 + dc * 128:
                                 par * DCH * 128 + (dc + 1) * 128],
                          wo_sb[:, dc * D + do2 * 512:dc * D + do2 * 512 + 512],
                          dc == 0, dc == DCH - 1)
            P.wait("vector", "PE", tpe)
            P.wait("vector", "IN", tin)
            dst = A_own[:, J * D + do2 * 512:J * D + do2 * 512 + 512]
            P.do("vector", lambda e, d=dst, s=psum[bank][:, 0:512],
                 x_=x_st3[:, par * D + do2 * 512:par * D + do2 * 512 + 512]:
                 e.tensor_tensor(d, s, x_, ALU.add))
            res_tkt[bank] = P.mark("vector", "DV")
        x3_dv[par] = P.cur("DV")
        wo_pe[par] = P.cur("PE")

    P.barrier()

    if stop == "P3":
        return _finish_dummy(cfg, P, H)
    carve.off = 0
    import math
    WCH = math.gcd(D, DI)
    wdraw = carve.f32(DI)
    bmxA4 = carve.f32(WCH // 2)
    bmxB4 = carve.f32(max(WCH // 4, 16))
    bs_s4 = carve.f32(WCH // 32)
    bs_i4 = carve.f32(WCH // 32)
    wdq_d = carve.bf16(DI)
    p4a_end = carve.off

    raw_free = 0
    for oc in range(DCH):
        P.wait("sync", "DV", raw_free)
        tin = _dma(P, "sync", "W0", wdraw, H["wd_h"][oc * 128:(oc + 1) * 128, :])
        for sub in range(DI // WCH):
            tdq = dequant(wdraw[:, sub * WCH:(sub + 1) * WCH], WCH // 32,
                          wdq_d[:, sub * WCH:(sub + 1) * WCH], tin,
                          bmxA4, bmxB4, bs_s4, bs_i4, "W0")
        raw_free = tdq
        P.wait("tensor", "DV", tdq)
        transpose_chunks(
            wdq_d, ICH,
            lambda ic, oc=oc: A_xn[:, ic * D + oc * 128:ic * D + (oc + 1) * 128])
    wdT_done = P.cur("DV")

    if stop == "P4a":
        return _finish_dummy(cfg, P, H)
    P.wait("gpsimd", "DV", ffn_mx_tkt)
    tcc = _collective(P, H, 1, mxu, mxg)
    P.wait("vector", "GP", tcc)
    ops23 = scale_ops(0, 0) + scale_ops(8, 2)
    t_scb23 = _bcast_scalars(P, H, ops23, 4, scb[:, 2:6], tp_free)
    su_b, invu_b = scb[:, 2:3], scb[:, 3:4]
    sg_b, invg_b = scb[:, 4:5], scb[:, 5:6]

    pe_wdT = P.cur("PE")
    carve.off = 0
    rb = carve.f32(2 * TG)
    gq_b = carve.f32(TG)
    t2_b = carve.f32(TG)
    G_b = carve.f32(TG)
    g2_b = carve.f32(2 * TG)
    p4b_end = carve.off

    P.wait("vector", "DV", t_scb23)
    P.wait("sync", "DV", raw_free)
    P.wait("sync", "PE", pe_wdT)
    P.wait("sync", "SP", P.cur("SP"))
    rb_dv = {}
    sp_tkt = {}
    first = True
    for ic in range(ICH):
        for tg_i in range(NTG):
            par = (ic * NTG + tg_i) % 2
            slot = rb[:, par * TG:(par + 1) * TG]
            if par in rb_dv:
                P.wait("sync", "DV", rb_dv[par])
            tin = _dma(P, "sync", f"R{par}", slot,
                       H["sp_g"][ic * 128:(ic + 1) * 128,
                                 tg_i * TG:(tg_i + 1) * TG])
            P.wait("vector", f"R{par}", tin)
            P.dve(lambda e, s=slot: e.tensor_scalar(
                gq_b, s, invg_b, MAGIC, ALU.mult, ALU.add), dep=True)
            P.dve(lambda e: e.tensor_scalar(
                gq_b, gq_b, MAGIC, sg_b, ALU.subtract, ALU.mult), dep=True)
            tq = P.mark("vector", "DV")
            rb_dv[par] = tq
            P.wait("scalar", "DV", tq)
            tsg = P.inc("scalar", lambda e: e.activation(
                t2_b, gq_b, ACTF.Sigmoid), "AC")
            P.wait("vector", "AC", tsg)
            P.dve(lambda e: e.tensor_scalar(
                t2_b, t2_b, 65536.0, MAGIC, ALU.mult, ALU.add), dep=True)
            P.dve(lambda e: e.tensor_scalar(
                G_b, t2_b, MAGIC, 65535.0, ALU.subtract, ALU.min), dep=True)
            g2slot = g2_b[:, par * TG:(par + 1) * TG]
            if par in sp_tkt:
                P.wait("vector", "SP", sp_tkt[par])
            P.dve(lambda e, d=g2slot: e.tensor_tensor(
                d, G_b, gq_b, ALU.mult), dep=True)
            P.dve(lambda e, d=g2slot: e.reduce_max(
                tmpm, d, AX.X, apply_absolute_value=True), dep=True)
            if first:
                P.dve(lambda e: e.tensor_copy(mx2, tmpm), dep=True)
                first = False
            else:
                P.dve(lambda e: e.tensor_tensor(mx2, mx2, tmpm, ALU.max),
                      dep=True)
            tsp = P.mark("vector", "DV")
            P.wait("scalar", "DV", tsp)
            sp_tkt[par] = _dma(P, "scalar", "SP",
                               H["sp_g2"][ic * 128:(ic + 1) * 128,
                                          tg_i * TG:(tg_i + 1) * TG], g2slot)
            if getattr(cfg, "s23_serial", False):
                P.barrier(flush_dma=True)
    t_mx2 = P.cur("DV")

    P.wait("gpsimd", "DV", t_mx2)
    tcc4 = _collective(P, H, 2, mx2)
    P.wait("vector", "GP", tcc4)
    ops4 = [lambda e: e.tensor_scalar_mul(rowb[0:1, 0:1], rowb[0:1, 0:1],
                                          1.0 / 65536.0),
            lambda e: e.drain()]
    ops4 += scale_ops(0, 0)
    ops4 += [lambda e: e.tensor_scalar_mul(rowb[0:1, 1:2], rowb[0:1, 1:2],
                                           1.0 / 65536.0),
             lambda e: e.drain()]
    t_scb4 = _bcast_scalars(P, H, ops4, 2, scb[:, 6:8], tp_free)
    s2_b, inv2_b = scb[:, 6:7], scb[:, 7:8]

    if stop == "S2":
        return _finish_dummy(cfg, P, H)
    rbu = carve.f32(2 * TG)
    uq_b = carve.f32(TG)
    op_b = carve.f32(2 * TG)
    p4c_end = carve.off

    P.wait("vector", "DV", t_scb4)
    P.wait("sync", "SP", P.cur("SP"))
    for pkey in list(sp_tkt):
        P.wait("vector", "SP", sp_tkt[pkey])
    spo_tkt = {}
    first = True
    for ic in range(ICH):
        for tg_i in range(NTG):
            par = (ic * NTG + tg_i) % 2
            slot_g2 = rb[:, par * TG:(par + 1) * TG]
            slot_u = rbu[:, par * TG:(par + 1) * TG]
            if par in rb_dv:
                P.wait("sync", "DV", rb_dv[par])
            tg2 = _dma(P, "sync", f"R{par}", slot_g2,
                       H["sp_g2"][ic * 128:(ic + 1) * 128,
                                  tg_i * TG:(tg_i + 1) * TG])
            tin = _dma(P, "sync", f"R{par + 2}", slot_u,
                       H["sp_up"][ic * 128:(ic + 1) * 128,
                                  tg_i * TG:(tg_i + 1) * TG])
            P.wait("vector", f"R{par}", tg2)
            P.wait("vector", f"R{par + 2}", tin)
            P.dve(lambda e, s=slot_u: e.tensor_scalar(
                uq_b, s, invu_b, MAGIC, ALU.mult, ALU.add), dep=True)
            P.dve(lambda e: e.tensor_scalar(
                uq_b, uq_b, MAGIC, su_b, ALU.subtract, ALU.mult), dep=True)
            P.dve(lambda e, s=slot_g2: e.tensor_scalar(
                s, s, inv2_b, MAGIC, ALU.mult, ALU.add), dep=True)
            P.dve(lambda e, s=slot_g2: e.tensor_scalar(
                s, s, MAGIC, s2_b, ALU.subtract, ALU.mult), dep=True)
            opslot = op_b[:, par * TG:(par + 1) * TG]
            if par in spo_tkt:
                P.wait("vector", "SP", spo_tkt[par])
            P.dve(lambda e, d=opslot, s=slot_g2: e.tensor_tensor(
                d, s, uq_b, ALU.mult), dep=True)
            P.dve(lambda e, d=opslot: e.reduce_max(
                tmpm, d, AX.X, apply_absolute_value=True), dep=True)
            if first:
                P.dve(lambda e: e.tensor_copy(mxo, tmpm), dep=True)
                first = False
            else:
                P.dve(lambda e: e.tensor_tensor(mxo, mxo, tmpm, ALU.max),
                      dep=True)
            tsp = P.mark("vector", "DV")
            rb_dv[par] = tsp
            P.wait("scalar", "DV", tsp)
            spo_tkt[par] = _dma(P, "scalar", "SP",
                                H["sp_o"][ic * 128:(ic + 1) * 128,
                                          tg_i * TG:(tg_i + 1) * TG], opslot)
            if getattr(cfg, "s23_serial", False):
                P.barrier(flush_dma=True)
    t_mxo = P.cur("DV")

    P.wait("gpsimd", "DV", t_mxo)
    tcc5 = _collective(P, H, 3, mxo)
    P.wait("vector", "GP", tcc5)
    t_scb5 = _bcast_scalars(P, H, scale_ops(0, 0), 2, scb[:, 8:10], tp_free)
    so_b, invo_b = scb[:, 8:9], scb[:, 9:10]

    P.barrier()

    if stop == "S3":
        P.wait("sync", "SP", P.cur("SP"))
        tpd = 0
        slot_i = 0
        for ic_d in (0, 5, 7, 11, 12, 17, 21, 23):
            for tg_d in (0, 2):
                tpd = _dma(P, "sync", "IN",
                           H["A_own"][:, D + slot_i * TG:D + (slot_i + 1) * TG],
                           H["sp_o"][ic_d * 128:(ic_d + 1) * 128,
                                     tg_d * TG:(tg_d + 1) * TG])
                slot_i += 1
        P.wait("sync", "IN", tpd)
        return _finish_dummy(cfg, P, H)
    carve.off = 0
    rbo = carve.f32(2 * TG)
    oq_scr = carve.f32(TG)

    P.wait("vector", "DV", t_scb5)
    P.wait("sync", "DV", t_mxo)
    P.wait("sync", "SP", P.cur("SP"))
    rbo_dv = {}
    dbank = {}
    tout = 0
    oar_pe = 0
    for tg_i in range(NTG):
        for ic in range(ICH):
            par = (tg_i * ICH + ic) % 2
            slot = rbo[:, par * TG:(par + 1) * TG]
            if par in rbo_dv:
                P.wait("sync", "DV", rbo_dv[par])
            tin = _dma(P, "sync", f"R{par}", slot,
                       H["sp_o"][ic * 128:(ic + 1) * 128,
                                 tg_i * TG:(tg_i + 1) * TG])
            P.wait("vector", f"R{par}", tin)
            if ic == 0:
                P.wait("vector", "PE", oar_pe)
            P.dve(lambda e, s=slot: e.tensor_scalar(
                oq_scr, s, invo_b, MAGIC, ALU.mult, ALU.add), dep=True)
            dst = A_q[:, ic * TG:(ic + 1) * TG]
            P.dve(lambda e, d=dst: e.tensor_scalar(
                d, oq_scr, MAGIC, so_b, ALU.subtract, ALU.mult), dep=True)
            rbo_dv[par] = P.mark("vector", "DV")
        t_oar = P.cur("DV")
        if getattr(cfg, "p5_serial", False):
            P.barrier(flush_dma=True)
        P.wait("tensor", "DV", t_oar)
        if tg_i == 0:
            P.wait("tensor", "DV", wdT_done)
        for tt_ in range(TG // 128):
            J = tg_i * (TG // 128) + tt_
            for do2 in range(D // 512):
                bank = (tt_ % 2) * 2 + do2
                if bank in dbank:
                    P.wait("tensor", "DV", dbank[bank])
                for ic in range(ICH):
                    tpe = _mm(P, psum[bank][:, 0:512],
                              A_q[:, ic * TG + tt_ * 128:ic * TG + tt_ * 128 + 128],
                              A_xn[:, ic * D + do2 * 512:ic * D + do2 * 512 + 512],
                              ic == 0, ic == ICH - 1)
                tpe = P.mark("tensor", "PE")
                P.wait("vector", "PE", tpe)
                dst = A_own[:, J * D + do2 * 512:J * D + do2 * 512 + 512]
                if getattr(cfg, "ffn_only", False):
                    P.do("vector", lambda e, d=dst, s=psum[bank][:, 0:512]:
                         e.tensor_copy(d, s))
                else:
                    P.do("vector", lambda e, d=dst, s=psum[bank][:, 0:512]:
                         e.tensor_tensor(d, d, s, ALU.add))
                dbank[bank] = P.mark("vector", "DV")
            for do2 in range(D // 512):
                P.wait("sync", "DV", dbank[(tt_ % 2) * 2 + do2])
            tout = _dma(P, "sync", "OUT", H["out_h"][J * 128:(J + 1) * 128, :],
                        A_own[:, J * D:(J + 1) * D])
        oar_pe = P.cur("PE")
        if getattr(cfg, "p5_serial", False):
            P.barrier(flush_dma=True)

    P.wait("sync", "OUT", tout)
    P.wait("gpsimd", "OUT", tout)


def _collective(P, H, rnd, mx_a, mx_b=None):
    rowb = H["row"]
    cin, cout = H["ccio"][rnd]
    ncores = H["cfg"].ncores
    P.do("gpsimd", lambda e, a=mx_a: e.tensor_reduce(
        rowb[0:1, 0:1], a, AX.C, ALU.max))
    if mx_b is not None:
        P.do("gpsimd", lambda e, a=mx_b: e.tensor_reduce(
            rowb[0:1, 8:9], a, AX.C, ALU.max))
    tg0 = P.mark("gpsimd", "GP")
    P.wait("vector", "GP", tg0)
    P.do("vector", lambda e: e.tensor_copy(
        rowb[0:1, 1:8], rowb[0:1, 0:1].broadcast_to((1, 7))))
    if mx_b is not None:
        P.do("vector", lambda e: e.tensor_copy(
            rowb[0:1, 9:16], rowb[0:1, 8:9].broadcast_to((1, 7))))
    else:
        P.do("vector", lambda e: e.tensor_copy(
            rowb[0:1, 8:16], rowb[0:1, 0:1].broadcast_to((1, 8))))
    tv0 = P.mark("vector", "DV")
    P.wait("gpsimd", "DV", tv0)
    t = _dma(P, "gpsimd", "WD", cin[:], rowb[0:1, 0:16])
    P.wait("gpsimd", "WD", t)
    tcc = P.inc("gpsimd",
                lambda e, ci=cin, co=cout, n=ncores: e.collective_compute(
                    "AllReduce", ALU.max, replica_groups=[list(range(n))],
                    ins=[ci[:]], outs=[co[:]]),
                "CC")
    P.wait("gpsimd", "CC", tcc)
    t2 = _dma(P, "gpsimd", "WD", rowb[0:1, 0:16], cout[:].unsqueeze(0))
    P.wait("gpsimd", "WD", t2)
    return P.mark("gpsimd", "GP")


def _bcast_scalars(P, H, dve_ops, n, scb_dst, tp_free):
    rowb, ones, pbank = H["row"], H["ones"], H["psum"][7]
    for f in dve_ops:
        P.do("vector", f)
    t = P.mark("vector", "DV")
    P.wait("tensor", "DV", t)
    P.wait("tensor", "DV", tp_free[1])
    _mm(P, pbank[:, 0:n], ones[:, :], rowb[0:1, 0:n], True, True)
    tp = P.mark("tensor", "PE")
    P.wait("vector", "PE", tp)
    P.do("vector", lambda e, d=scb_dst, s=pbank[:, 0:n]: e.tensor_copy(d, s))
    tkt = P.mark("vector", "DV")
    tp_free[1] = tkt
    return tkt



def make_masks(cfg, h):
    NJ, TQG = cfg.NJ, cfg.TQG
    tril = np.triu(np.ones((128, 128), np.float32))
    m = np.zeros((2 * NJ * 128, TQG), dtype=np.float32)
    for mi in range(2 * NJ):
        for jj in range(NJ):
            blk = m[mi * 128:(mi + 1) * 128, jj * 128:(jj + 1) * 128]
            thr = 2 * jj
            if mi < thr:
                blk[:] = 1.0
            elif mi == thr:
                blk[:] = tril
            elif mi == thr + 1 and h == 1:
                blk[:] = 1.0
    return m.astype(ml_dtypes.bfloat16)


def permute_x(x_b, h):
    if h == 0:
        return x_b
    S, D = x_b.shape
    return np.ascontiguousarray(
        x_b.reshape(S // 256, 2, 128, D)[:, ::-1].reshape(S, D))


def kernel(L0_attn_in_features, L0_attn_Wq, L0_attn_Wk, L0_attn_Wv, L0_attn_Wo,
           L0_ffn_w_gate, L0_ffn_w_up, L0_ffn_w_down):
    cfg = Config(4, 4096, 1024, 16, 3072, 8)
    f = lambda a: np.ascontiguousarray(np.asarray(a, dtype=np.float32))
    x = f(L0_attn_in_features)
    consts = dict(wq=f(L0_attn_Wq), wk=f(L0_attn_Wk), wv=f(L0_attn_Wv),
                  wo=f(L0_attn_Wo), wup=f(L0_ffn_w_up), wgate=f(L0_ffn_w_gate),
                  wdown=f(L0_ffn_w_down),
                  ident=np.eye(128, dtype=ml_dtypes.bfloat16),
                  ones=np.ones((1, 128), np.float32))
    masks = [make_masks(cfg, h) for h in (0, 1)]
    in_maps = []
    for c in range(cfg.ncores):
        b, h = c // 2, c % 2
        m = dict(consts)
        m["x"] = permute_x(x[b], h)
        m["masks"] = masks[h]
        in_maps.append(m)
    nc = build_nc(cfg)
    res = run_bass_kernel_spmd(nc, in_maps, core_ids=list(range(cfg.ncores)))
    out = np.empty_like(x)
    for c in range(cfg.ncores):
        b, h = c // 2, c % 2
        r = np.asarray(res.results[c]["out"])
        for j in range(cfg.NQ):
            g = (2 * j + h) * 128
            out[b, g:g + 128] = r[j * 128:(j + 1) * 128]
    return out


# version 10
# speedup vs baseline: 1.0612x; 1.0612x over previous
import numpy as np
import ml_dtypes
import concourse.bass as bass
import concourse.mybir as mybir
from concourse.bass_utils import run_bass_kernel_spmd
from contextlib import ExitStack

F32 = mybir.dt.float32
BF16 = mybir.dt.bfloat16
ALU = mybir.AluOpType
ACTF = mybir.ActivationFunctionType
AX = mybir.AxisListType
MAGIC = float(np.float32(1.5 * 2 ** 23))
QMAX = 32767.0
ENGS = ("sync", "vector", "scalar", "tensor", "gpsimd")


class Config:
    def __init__(self, B=4, S=4096, D=1024, NH=16, DI=3072, ncores=8):
        self.B, self.S, self.D, self.NH, self.DI, self.ncores = B, S, D, NH, DI, ncores
        self.HD = D // NH
        assert self.HD == 64 and NH % 2 == 0
        self.T = S // 2
        self.DCH = D // 128
        self.ICH = DI // 128
        self.STILES = S // 128
        self.NQ = self.T // 128
        self.TQG = min(512, self.T)
        assert self.TQG % 128 == 0 and self.T % self.TQG == 0
        self.NJ = self.TQG // 128
        self.NG = self.T // self.TQG
        self.TG = min(512, self.T)
        self.NTG = self.T // self.TG
        self.KTG = min(512, S)
        self.NPAIR = NH // 2
        assert S % 256 == 0 and D % 512 == 0 and DI % 128 == 0


class Prog:

    def __init__(self, sems):
        self.ops = {e: [] for e in ENGS}
        self.cnt = {}
        self.sems = sems
        self.bar_n = 0

    def do(self, eng, fn):
        self.ops[eng].append(["op", fn, None])

    def dve(self, fn, dep=False):
        if dep:
            self.ops["vector"].append(["op", lambda e: e.drain(), None])
        self.ops["vector"].append(["op", fn, None])

    def inc(self, eng, fn, sem, n=1):
        c = self.cnt.get(sem, 0) + n
        self.cnt[sem] = c
        if n == 1 and eng in ("vector", "scalar"):
            self.ops[eng].append(["op", fn, None])
            self.ops[eng].append(["op", lambda e: e.drain(), (sem, n)])
        else:
            self.ops[eng].append(["op", fn, (sem, n)])
        return c

    def mark(self, eng, sem):
        lst = self.ops[eng]
        if eng in ("vector", "scalar", "tensor", "gpsimd"):
            c = self.cnt.get(sem, 0) + 1
            self.cnt[sem] = c
            lst.append(["op", lambda e: e.drain(), (sem, 1)])
            return c
        last_op = None
        for item in reversed(lst):
            if item[0] == "op":
                last_op = item
                break
        if last_op is not None and last_op[2] is None:
            c = self.cnt.get(sem, 0) + 1
            self.cnt[sem] = c
            last_op[2] = (sem, 1)
        else:
            if last_op is not None:
                s2 = last_op[2][0]
                lst.append(["wait", s2, self.cnt.get(s2, 0)])
            c = self.cnt.get(sem, 0) + 1
            self.cnt[sem] = c
            lst.append(["sinc", sem, 1])
        return c

    def wait(self, eng, sem, val):
        if val <= 0:
            return
        self.ops[eng].append(["wait", sem, val])

    def cur(self, sem):
        return self.cnt.get(sem, 0)

    def barrier(self, flush_dma=False):
        if flush_dma:
            for s_ in ("IN", "SP", "WD", "OUT"):
                self.wait("sync", s_, self.cnt.get(s_, 0))
        self.bar_n += 1
        for e in ENGS:
            self.mark(e, "BR")
        for e in ENGS:
            self.wait(e, "BR", 5 * self.bar_n)


class Carver:
    def __init__(self, ap, n):
        self.ap, self.n, self.off = ap, n, 0

    def f32(self, n):
        assert self.off + n <= self.n, f"misc overflow {self.off}+{n}>{self.n}"
        s = self.ap[:, self.off:self.off + n]
        self.off += n
        return s

    def bf16(self, n):
        assert n % 2 == 0
        return self.f32(n // 2).bitcast(BF16)


def build_nc(cfg):
    nc = bass.Bass("TRN2", target_bir_lowering=False, debug=False,
                   detect_race_conditions=False)
    S, D, T, DI = cfg.S, cfg.D, cfg.T, cfg.DI
    NJ, TQG = cfg.NJ, cfg.TQG

    H = {"cfg": cfg}
    H["x_h"] = nc.dram_tensor("x", [S, D], F32, kind="ExternalInput").ap()
    for nm in ("wq", "wk", "wv", "wo"):
        H[nm + "_h"] = nc.dram_tensor(nm, [D, D], F32, kind="ExternalInput").ap()
    H["wup_h"] = nc.dram_tensor("wup", [DI, D], F32, kind="ExternalInput").ap()
    H["wg_h"] = nc.dram_tensor("wgate", [DI, D], F32, kind="ExternalInput").ap()
    H["wd_h"] = nc.dram_tensor("wdown", [D, DI], F32, kind="ExternalInput").ap()
    H["ident_h"] = nc.dram_tensor("ident", [128, 128], BF16, kind="ExternalInput").ap()
    H["ones_h"] = nc.dram_tensor("ones", [1, 128], F32, kind="ExternalInput").ap()
    H["masks_h"] = nc.dram_tensor("masks", [2 * NJ * 128, TQG], BF16,
                                  kind="ExternalInput").ap()
    H["out_h"] = nc.dram_tensor("out", [T, D], F32, kind="ExternalOutput").ap()
    for nm in ("sp_up", "sp_g", "sp_g2", "sp_o"):
        H[nm] = nc.dram_tensor(nm, [DI, T], F32).ap()
    H["ccio"] = [(nc.dram_tensor(f"cc_in{r}", [16], F32).ap(),
                  nc.dram_tensor(f"cc_out{r}", [16], F32).ap()) for r in range(4)]

    es = ExitStack()
    with es:
        sbt = lambda name, shape, dt: es.enter_context(
            nc.sbuf_tensor(name, shape, dt))
        H["A_xn"] = sbt("A_xn", [128, max(cfg.DCH * S, cfg.ICH * D)], BF16)[:]
        H["A_own"] = sbt("A_own", [128, cfg.NQ * D], F32)[:]
        H["A_q"] = sbt("A_q", [128, max(cfg.DCH * T, cfg.NQ * D,
                                        cfg.ICH * cfg.TG)], BF16)[:]
        MISCN = 8064
        H["A_misc"] = sbt("A_misc", [128, MISCN], F32)[:]
        H["scr"] = sbt("scr", [128, D], F32)[:]
        H["ident"] = sbt("identS", [128, 128], BF16)[:]
        H["ones"] = sbt("onesS", [1, 128], F32)[:]
        H["masks"] = sbt("masksS", [128, 2 * NJ * TQG], BF16)[:]
        H["stat"] = sbt("stat", [128, 16], F32)[:]
        H["mxs"] = sbt("mxs", [128, 8], F32)[:]
        H["scb"] = sbt("scb", [128, 12], F32)[:]
        H["row"] = sbt("rowS", [1, 32], F32)[:]
        H["cst"] = sbt("cstS", [128, 4], F32)[:]
        H["psum"] = [es.enter_context(nc.psum_tensor(f"pb{i}", [128, 512], F32))
                     for i in range(8)]
        sems = {}
        for s_ in ("IN", "WD", "SP", "OUT", "PE", "DV", "AC", "GP", "CC", "BR",
                   "R0", "R1", "R2", "R3", "X0", "X1", "W0", "W1"):
            sems[s_] = es.enter_context(nc.semaphore(s_))
        P = Prog(sems)
        H["carve"] = Carver(H["A_misc"], MISCN)

        plan(cfg, P, H)

        block = es.enter_context(nc.Block())

        def runner(ops):
            def body(e):
                for item in ops:
                    kind = item[0]
                    if kind == "op":
                        inst = item[1](e)
                        if item[2] is not None:
                            sem, n = item[2]
                            inst.then_inc(sems[sem], n)
                    elif kind == "wait":
                        e.wait_ge(sems[item[1]], item[2])
                    else:
                        e.sem_inc(sems[item[1]], item[2])
            return body
        block.sync(runner(P.ops["sync"]))
        block.vector(runner(P.ops["vector"]))
        block.scalar(runner(P.ops["scalar"]))
        block.tensor(runner(P.ops["tensor"]))
        block.gpsimd(runner(P.ops["gpsimd"]))
    return nc


def _dma(P, eng, sem, out, in_):
    return P.inc(eng, lambda e, o=out, i=in_: e.dma_start(out=o, in_=i), sem, 16)


def _mm(P, out, lhsT, rhs, start, stop, tp=None, skip=False):
    return P.inc("tensor",
                 lambda e, o=out, l=lhsT, r=rhs, st=start, sp=stop, t=tp, sk=skip:
                 e.matmul(o, l, r, start=st, stop=sp, tile_position=t,
                          skip_group_check=sk),
                 "PE")



def _finish_dummy(cfg, P, H):
    D, NQ = cfg.D, cfg.NQ
    t = P.cur("DV")
    P.wait("sync", "DV", t)
    P.wait("sync", "AC", P.cur("AC"))
    P.wait("sync", "GP", P.cur("GP"))
    tout = 0
    for J in range(1, NQ):
        tout = _dma(P, "sync", "OUT", H["out_h"][J * 128:(J + 1) * 128, :],
                    H["A_own"][:, J * D:(J + 1) * D])
    tout = _dma(P, "sync", "OUT", H["out_h"][0:128, 0:12], H["scb"][:, :])
    tout = _dma(P, "sync", "OUT", H["out_h"][0:128, 16:24], H["mxs"][:, :])
    tout = _dma(P, "sync", "OUT", H["out_h"][0:128, 32:48], H["stat"][:, :])
    tout = _dma(P, "sync", "OUT", H["out_h"][0:1, 64:96], H["row"][:, :])
    P.wait("sync", "OUT", tout)
    P.wait("gpsimd", "OUT", tout)


def plan(cfg, P, H):
    S, D, T, DI = cfg.S, cfg.D, cfg.T, cfg.DI
    DCH, ICH, NJ, TQG = cfg.DCH, cfg.ICH, cfg.NJ, cfg.TQG
    TG, NTG, NQ, STILES = cfg.TG, cfg.NTG, cfg.NQ, cfg.STILES
    NPAIR, KTG, NG = cfg.NPAIR, cfg.KTG, cfg.NG
    carve, psum = H["carve"], H["psum"]
    A_xn, A_own, A_q, scr = H["A_xn"], H["A_own"], H["A_q"], H["scr"]
    stat, mxs, scb, rowb = H["stat"], H["mxs"], H["scb"], H["row"]
    ident, masks, ones = H["ident"], H["masks"], H["ones"]
    eps = 1e-5
    stop = getattr(cfg, "stop_after", None)
    if stop == "MEMS":
        P.do("vector", lambda e: e.memset(A_own[:, :], 7.0))
        P.mark("vector", "DV")
        return _finish_dummy(cfg, P, H)
    if stop == "XDUMP":
        x_stage0 = carve.f32(D)
        tin0 = _dma(P, "sync", "IN", x_stage0, H["x_h"][0:128, :])
        P.wait("vector", "IN", tin0)
        P.do("vector", lambda e: e.tensor_copy(A_own[:, D:2 * D], x_stage0))
        P.mark("vector", "DV")
        carve.off = 0
        return _finish_dummy(cfg, P, H)

    mean, var, rstd = stat[:, 0:1], stat[:, 1:2], stat[:, 2:3]
    t0, t1, nbias, tmpm = stat[:, 3:4], stat[:, 4:5], stat[:, 5:6], stat[:, 6:7]
    mx_ln, mxu, mxg, mx2, mxo = (mxs[:, i:i + 1] for i in range(5))

    cst = H["cst"]
    for i_, v_ in enumerate((-1.0, MAGIC, 1.0 / 7.0, 1e-12)):
        P.do("vector", lambda e, i=i_, v=v_: e.memset(cst[:, i:i + 1], v))
    P.mark("vector", "DV")
    _dma(P, "sync", "IN", ident[:, :], H["ident_h"][:, :])
    t_ones = _dma(P, "sync", "IN", ones[:, :], H["ones_h"][:, :])
    P.wait("tensor", "IN", t_ones)
    for mi in range(2 * NJ):
        tconst = _dma(P, "sync", "IN", masks[:, mi * TQG:(mi + 1) * TQG],
                      H["masks_h"][mi * 128:(mi + 1) * 128, :])

    tp_free = [0, 0]

    def transpose_chunks(src, nch, dst_fn, grouped_dst=None):
        for half in range((nch + 3) // 4):
            bank = 6 + (half % 2)
            P.wait("tensor", "DV", tp_free[half % 2])
            ndc = min(4, nch - half * 4)
            for i in range(ndc):
                c = half * 4 + i
                tpe = _mm(P, psum[bank][:, i * 128:(i + 1) * 128],
                          src[:, c * 128:(c + 1) * 128], ident[:, :],
                          True, True)
            tpe = P.mark("tensor", "PE")
            P.wait("vector", "PE", tpe)
            if grouped_dst is not None:
                dsts = [(grouped_dst(half, ndc),
                         psum[bank][:, 0:ndc * 128].rearrange(
                             "p (c t) -> p c t", t=128))]
            else:
                dsts = [(dst_fn(half * 4 + i),
                         psum[bank][:, i * 128:(i + 1) * 128])
                        for i in range(ndc)]
            for d, s_ in dsts:
                P.do("vector", lambda e, d=d, s=s_: e.tensor_copy(d, s))
            tp_free[half % 2] = P.mark("vector", "DV")

    x_stage = carve.f32(2 * D)
    xnbf = carve.bf16(2 * D)

    ln_max_tkt = 0
    n_own_seen = 0
    last_act = 0
    xs_ac = [0, 0]
    xb_pe = [0, 0]
    for pt in range(STILES):
        par = pt % 2
        is_own = (pt % 2 == 0)
        j = pt // 2
        xs = x_stage[:, par * D:(par + 1) * D]
        P.wait("sync", "AC", xs_ac[par])
        tin = _dma(P, "sync", f"X{par}", xs, H["x_h"][pt * 128:(pt + 1) * 128, :])
        P.wait("vector", f"X{par}", tin)
        P.wait("vector", "AC", last_act)
        P.do("vector", lambda e, xs=xs: e.reduce_sum(mean, xs, AX.X))
        P.dve(lambda e: e.tensor_scalar_mul(mean, mean, 1.0 / D), dep=True)
        P.dve(lambda e, xs=xs: e.scalar_tensor_tensor(
            scr[:, 0:D], xs, mean[:, 0:1], xs, ALU.subtract, ALU.mult), dep=True)
        P.dve(lambda e: e.reduce_sum(var, scr[:, 0:D], AX.X), dep=True)
        P.dve(lambda e: e.tensor_scalar(var, var, 1.0 / D, eps,
                                        ALU.mult, ALU.add), dep=True)
        P.dve(lambda e: e.reciprocal(t0, var), dep=True)
        td = P.mark("vector", "DV")
        P.wait("scalar", "DV", td)
        ts_ = P.inc("scalar", lambda e: e.activation(rstd, t0, ACTF.Sqrt), "AC")
        P.wait("vector", "AC", ts_)
        P.dve(lambda e: e.tensor_tensor(t1, rstd, rstd, ALU.mult), dep=True)
        P.dve(lambda e: e.tensor_tensor(t1, t1, var, ALU.mult), dep=True)
        P.dve(lambda e: e.tensor_scalar(t1, t1, -0.5, 1.5,
                                        ALU.mult, ALU.add), dep=True)
        P.dve(lambda e: e.tensor_tensor(rstd, rstd, t1, ALU.mult), dep=True)
        P.dve(lambda e: e.scalar_tensor_tensor(
            nbias, mean, -1.0, rstd, ALU.mult, ALU.mult), dep=True)
        tdv = P.mark("vector", "DV")
        P.wait("scalar", "DV", tdv)
        P.wait("scalar", "PE", xb_pe[par])
        xb = xnbf[:, par * D:(par + 1) * D]
        if is_own:
            dst = A_own[:, j * D:(j + 1) * D]
            last_act = P.inc("scalar", lambda e, xs=xs, d=dst: e.activation(
                d, xs, ACTF.Identity, bias=nbias[:, 0:1], scale=rstd[:, 0:1]),
                "AC")
            xs_ac[par] = last_act
            P.wait("vector", "AC", last_act)
            P.wait("vector", "PE", xb_pe[par])
            P.do("vector", lambda e, d=dst, xb=xb: e.tensor_copy(xb, d))
            P.do("vector", lambda e, d=dst: e.reduce_max(
                tmpm, d, AX.X, apply_absolute_value=True))
            if n_own_seen == 0:
                P.dve(lambda e: e.tensor_copy(mx_ln, tmpm), dep=True)
            else:
                P.dve(lambda e: e.tensor_tensor(mx_ln, mx_ln, tmpm,
                                                ALU.max), dep=True)
            n_own_seen += 1
            if n_own_seen == NQ:
                ln_max_tkt = P.mark("vector", "DV")
        else:
            last_act = P.inc("scalar", lambda e, xs=xs, xb=xb: e.activation(
                xb, xs, ACTF.Identity, bias=nbias[:, 0:1], scale=rstd[:, 0:1]),
                "AC")
            xs_ac[par] = last_act
        tb = P.mark("vector", "DV")
        if not is_own:
            P.wait("tensor", "AC", last_act)
        P.wait("tensor", "DV", tb)
        transpose_chunks(
            xb, DCH, None,
            grouped_dst=lambda half, ndc, pt=pt: A_xn[:].rearrange(
                "p (c s) -> p c s", s=S)[:, half * 4:half * 4 + ndc,
                                         pt * 128:(pt + 1) * 128])
        xb_pe[par] = P.cur("PE")
        if stop == f"LNT{pt}":
            return _finish_dummy(cfg, P, H)

    if stop == "LN":
        return _finish_dummy(cfg, P, H)
    P.wait("gpsimd", "DV", ln_max_tkt)
    tcc = _collective(P, H, 0, mx_ln)
    P.wait("vector", "GP", tcc)
    if stop == "CC1":
        return _finish_dummy(cfg, P, H)

    def scale_ops(src_i, dst_i):
        a = rowb[0:1, src_i:src_i + 1]
        s_ = rowb[0:1, dst_i:dst_i + 1]
        iv = rowb[0:1, dst_i + 1:dst_i + 2]
        return [
            lambda e, a=a, s_=s_: e.tensor_single_scalar(s_, a, 1.0 / QMAX,
                                                         ALU.mult),
            lambda e: e.drain(),
            lambda e, s_=s_: e.tensor_scalar_max(s_, s_, 1e-12),
            lambda e: e.drain(),
            lambda e, s_=s_, iv=iv: e.reciprocal(iv, s_),
            lambda e: e.drain(),
        ]
    t_scb1 = _bcast_scalars(P, H, scale_ops(0, 0), 2, scb[:, 0:2], tp_free)
    s1b, inv1b = scb[:, 0:1], scb[:, 1:2]
    if stop == "BC1":
        return _finish_dummy(cfg, P, H)

    xqbf = xnbf
    P.wait("vector", "DV", t_scb1)
    for j in range(NQ):
        par = j % 2
        src = A_own[:, j * D:(j + 1) * D]
        qb = xqbf[:, par * D:(par + 1) * D]
        P.wait("vector", "PE", xb_pe[par])
        if True:
            P.dve(lambda e, s=src: e.tensor_scalar(
                scr[:, 0:D], s, inv1b, MAGIC, ALU.mult, ALU.add), dep=True)
            P.dve(lambda e, q=qb: e.tensor_scalar(
                q, scr[:, 0:D], MAGIC, s1b, ALU.subtract, ALU.mult), dep=True)
        tb = P.mark("vector", "DV")
        P.wait("tensor", "DV", tb)
        if getattr(cfg, "xq_no_tp", False) and j >= 2:
            xb_pe[par] = P.cur("PE")
            if stop == f"XQ{j}":
                return _finish_dummy(cfg, P, H)
            continue
        transpose_chunks(
            qb, DCH, None,
            grouped_dst=lambda half, ndc, j=j: A_q[:].rearrange(
                "p (c t) -> p c t", t=T)[:, half * 4:half * 4 + ndc,
                                         j * 128:(j + 1) * 128])
        xb_pe[par] = P.cur("PE")
        if stop == f"XQ{j}":
            return _finish_dummy(cfg, P, H)

    P.barrier()

    if stop == "P0":
        return _finish_dummy(cfg, P, H)
    carve.off = 0
    wupraw = carve.f32(D)
    wgraw = carve.f32(D)
    stg = carve.f32(4 * TG)
    bmxA = carve.f32(D // 2)
    bmxB = carve.f32(max(D // 4, 16))
    bs_s = carve.f32(D // 32)
    bs_i = carve.f32(D // 32)
    wdqu = carve.bf16(D)
    wdqg = carve.bf16(D)
    wTu = carve.bf16(2 * DCH * 128)
    wTg = carve.bf16(2 * DCH * 128)

    def dequant(raw, nblk, wdq, tin, bA, bB, ss, si, insem="IN"):
        W = nblk * 32
        P.wait("vector", insem, tin)
        P.do("vector", lambda e, r=raw, s=scr[:, 0:W]: e.scalar_tensor_tensor(
            s, r, -1.0, r, ALU.mult, ALU.max))
        aw = scr[:, 0:W].rearrange("p (b k) -> p b k", k=32)
        P.dve(lambda e, a=aw, n=nblk: e.tensor_tensor(
            bA[:, 0:n * 16].rearrange("p (b k) -> p b k", k=16),
            a[:, :, 0:16], a[:, :, 16:32], ALU.max), dep=True)
        cur, w, i = bA[:, 0:nblk * 16].rearrange("p (b k) -> p b k", k=16), 16, 0
        while w > 2:
            hw = w // 2
            nxt = [bB, bA][i % 2][:, 0:nblk * hw].rearrange(
                "p (b k) -> p b k", k=hw)
            P.dve(lambda e, n=nxt, c=cur, h=hw, w=w: e.tensor_tensor(
                n, c[:, :, 0:h], c[:, :, h:w], ALU.max), dep=True)
            cur, w, i = nxt, hw, i + 1
        P.dve(lambda e, c=cur, n=nblk, ss=ss: e.tensor_tensor(
            ss[:, 0:n], c[:, :, 0:1].squeeze(2), c[:, :, 1:2].squeeze(2),
            ALU.max), dep=True)
        P.dve(lambda e, n=nblk, ss=ss: e.tensor_scalar(
            ss[:, 0:n], ss[:, 0:n], 1.0 / 7.0, 1e-12, ALU.mult, ALU.max),
            dep=True)
        P.dve(lambda e, n=nblk, ss=ss, si=si: e.reciprocal(
            si[:, 0:n], ss[:, 0:n]), dep=True)
        ib = si[:, 0:nblk].unsqueeze(2).broadcast_to((128, nblk, 32))
        sbb = ss[:, 0:nblk].unsqueeze(2).broadcast_to((128, nblk, 32))
        a = raw.rearrange("p (b k) -> p b k", k=32)
        s3 = scr[:, 0:W].rearrange("p (b k) -> p b k", k=32)
        P.dve(lambda e, a=a, ib=ib, s=s3: e.tensor_tensor(
            s, a, ib, ALU.mult), dep=True)
        P.dve(lambda e, s=scr[:, 0:W]: e.tensor_scalar(
            s, s, MAGIC, MAGIC, ALU.add, ALU.subtract), dep=True)
        P.dve(lambda e, s=s3, sbb=sbb,
             d=wdq.rearrange("p (b k) -> p b k", k=32): e.scalar_tensor_tensor(
            d, s, 1.0, sbb, ALU.mult, ALU.mult), dep=True)
        return P.mark("vector", "DV")

    bank_guard = {}
    stg_dma = {}
    stg_dv = {}
    mx_first = {"u": True, "g": True}
    wT_read = [0, 0]
    raw_free = 0
    for ic in range(ICH):
        par = ic % 2
        P.wait("sync", "DV", raw_free)
        tinU = _dma(P, "sync", "W0", wupraw, H["wup_h"][ic * 128:(ic + 1) * 128, :])
        tinG = _dma(P, "sync", "W1", wgraw, H["wg_h"][ic * 128:(ic + 1) * 128, :])
        tqU = dequant(wupraw, D // 32, wdqu, tinU, bmxA, bmxB, bs_s, bs_i,
                      "W0")
        tqG = dequant(wgraw, D // 32, wdqg, tinG, bmxA, bmxB, bs_s, bs_i,
                      "W1")
        raw_free = tqG
        P.wait("tensor", "DV", tqU)
        P.wait("vector", "PE", wT_read[par])
        transpose_chunks(wdqu, DCH,
                         lambda c, par=par: wTu[:, par * DCH * 128 + c * 128:
                                                par * DCH * 128 + (c + 1) * 128])
        P.wait("tensor", "DV", tqG)
        transpose_chunks(wdqg, DCH,
                         lambda c, par=par: wTg[:, par * DCH * 128 + c * 128:
                                                par * DCH * 128 + (c + 1) * 128])
        tT = P.cur("DV")
        if stop == f"P1T{ic}":
            return _finish_dummy(cfg, P, H)
        P.wait("tensor", "DV", tT)
        for w_i, (wT, bpair, spill, mxc, mk) in enumerate((
                (wTu, (0, 1), H["sp_up"], mxu, "u"),
                (wTg, (2, 3), H["sp_g"], mxg, "g"))):
            for tg_i in range(NTG):
                bank = bpair[tg_i % 2]
                for sem, tk in bank_guard.get(bank, ()):
                    P.wait("tensor", sem, tk)
                for dc in range(DCH):
                    tpe = _mm(P, psum[bank][:, 0:TG],
                              wT[:, par * DCH * 128 + dc * 128:
                                 par * DCH * 128 + (dc + 1) * 128],
                              A_q[:, dc * T + tg_i * TG:dc * T + (tg_i + 1) * TG],
                              dc == 0, dc == DCH - 1)
                slot = stg[:, (w_i * 2 + tg_i % 2) * TG:
                           (w_i * 2 + tg_i % 2 + 1) * TG]
                key = (w_i, tg_i % 2)
                tpe = P.mark("tensor", "PE")
                P.wait("scalar", "PE", tpe)
                if key in stg_dma:
                    P.wait("scalar", "SP", stg_dma[key])
                if key in stg_dv:
                    P.wait("scalar", "DV", stg_dv[key])
                tev = P.inc("scalar", lambda e, d=slot, s=psum[bank][:, 0:TG]:
                            e.activation(d, s, ACTF.Identity), "AC")
                bank_guard[bank] = [("AC", tev)]
                P.wait("vector", "AC", tev)
                P.do("vector", lambda e, s=slot: e.reduce_max(
                    tmpm, s, AX.X, apply_absolute_value=True))
                if mx_first[mk]:
                    P.dve(lambda e, m=mxc: e.tensor_copy(m, tmpm), dep=True)
                    mx_first[mk] = False
                else:
                    P.dve(lambda e, m=mxc: e.tensor_tensor(
                        m, m, tmpm, ALU.max), dep=True)
                stg_dv[key] = P.mark("vector", "DV")
                P.wait("scalar", "AC", tev)
                dst = spill[ic * 128:(ic + 1) * 128, tg_i * TG:(tg_i + 1) * TG]
                stg_dma[key] = _dma(P, "scalar", "SP", dst, slot)
        wT_read[par] = P.cur("PE")
        if stop == f"P1C{ic}":
            return _finish_dummy(cfg, P, H)
    ffn_mx_tkt = P.cur("DV")

    P.barrier()

    if stop == "P1":
        P.wait("sync", "SP", P.cur("SP"))
        slot_i = 0
        for sp in (H["sp_up"], H["sp_g"]):
            for ic_d in (0, 12):
                for tg_d in range(NTG):
                    tpd = _dma(P, "sync", "IN",
                               H["A_own"][:, D + slot_i * TG:D + (slot_i + 1) * TG],
                               sp[ic_d * 128:(ic_d + 1) * 128,
                                  tg_d * TG:(tg_d + 1) * TG])
                    slot_i += 1
        P.wait("sync", "IN", tpd)
        return _finish_dummy(cfg, P, H)
    carve.off = 0
    wqkv = carve.bf16(3 * DCH * 128)
    K_hp = carve.bf16(S)
    Q_hp = carve.bf16(T)
    V_hp = carve.bf16(STILES * 130)
    Pt = carve.bf16(4 * TQG)
    rden = carve.f32(2)

    P.do("vector", lambda e: e.memset(
        V_hp.rearrange("p (t c) -> p t c", c=130)[:, :, 64:130:65], 1.0))
    P.mark("vector", "DV")

    sc_free = {}
    pt_free = {}
    ctx_free = [0, 0]
    kqv_pe = 0
    wqkv_pe = 0
    kqv_dv = 0

    for hp in range(NPAIR):
        par = hp % 2
        base = 0
        P.wait("gpsimd", "PE", wqkv_pe)
        tw = 0
        for w_i, w_h in enumerate((H["wk_h"], H["wq_h"], H["wv_h"])):
            for dc in range(DCH):
                dst = wqkv[:, base + w_i * DCH * 128 + dc * 128:
                           base + w_i * DCH * 128 + (dc + 1) * 128]
                src = w_h[dc * 128:(dc + 1) * 128, hp * 128:(hp + 1) * 128]
                tw = P.inc("gpsimd", lambda e, d=dst, s=src:
                           e.dma_start(out=d, in_=s), "WD", 16)
        kt = wqkv[:, base:base + DCH * 128]
        qt = wqkv[:, base + DCH * 128:base + 2 * DCH * 128]
        vt = wqkv[:, base + 2 * DCH * 128:base + 3 * DCH * 128]
        P.wait("tensor", "WD", tw)
        P.wait("tensor", "PE", kqv_pe)
        P.wait("vector", "PE", kqv_pe)
        pb = [0, 0]
        for kg in range(S // KTG):
            bank = 6 + kg % 2
            P.wait("tensor", "DV", pb[kg % 2])
            for dc in range(DCH):
                tpe = _mm(P, psum[bank][:, 0:KTG],
                          kt[:, dc * 128:(dc + 1) * 128],
                          A_xn[:, dc * S + kg * KTG:dc * S + (kg + 1) * KTG],
                          dc == 0, dc == DCH - 1)
            tpe = P.mark("tensor", "PE")
            P.wait("vector", "PE", tpe)
            P.do("vector", lambda e, d=K_hp[:, kg * KTG:(kg + 1) * KTG],
                 s=psum[bank][:, 0:KTG]: e.tensor_copy(d, s))
            pb[kg % 2] = P.mark("vector", "DV")
        QG = min(KTG, T)
        for qg in range(T // QG):
            bank = 6 + qg % 2
            P.wait("tensor", "DV", pb[qg % 2])
            ntq = QG // 128
            for dc in range(DCH):
                src = A_xn[:, dc * S + qg * 2 * QG:
                           dc * S + (qg + 1) * 2 * QG].rearrange(
                    "p (t q) -> p t q", q=256)[:, :, 0:128]
                tpe = _mm(P, psum[bank][:, 0:QG],
                          qt[:, dc * 128:(dc + 1) * 128], src,
                          dc == 0, dc == DCH - 1)
            tpe = P.mark("tensor", "PE")
            P.wait("vector", "PE", tpe)
            P.do("vector", lambda e, d=Q_hp[:, qg * QG:(qg + 1) * QG],
                 s=psum[bank][:, 0:QG]: e.tensor_copy(d, s))
            pb[qg % 2] = P.mark("vector", "DV")
        for vg in range((STILES + 3) // 4):
            bank = 6 + vg % 2
            P.wait("tensor", "DV", pb[vg % 2])
            nvt = min(4, STILES - vg * 4)
            for m in range(nvt):
                vti = vg * 4 + m
                for dc in range(DCH):
                    tpe = _mm(P, psum[bank][:, m * 128:m * 128 + 128],
                              A_xn[:, dc * S + vti * 128:dc * S + (vti + 1) * 128],
                              vt[:, dc * 128:(dc + 1) * 128],
                              dc == 0, dc == DCH - 1)
            tpe = P.mark("tensor", "PE")
            P.wait("vector", "PE", tpe)
            src = psum[bank][:, 0:nvt * 128].rearrange(
                "p (m h d) -> p m h d", h=2, d=64)
            dst = V_hp[:, vg * 4 * 130:(vg * 4 + nvt) * 130].rearrange(
                "p (m c) -> p m c", c=130).rearrange(
                "p m (h d) -> p m h d", h=2, d=65)[:, :, :, 0:64]
            P.do("vector", lambda e, d=dst, s=src: e.tensor_copy(d, s))
            pb[vg % 2] = P.mark("vector", "DV")
        kqv_dv = P.cur("DV")
        wqkv_pe = P.cur("PE")
        P.wait("tensor", "DV", kqv_dv)

        for G in range(NG):
            KT = 2 * NJ * (G + 1)
            m0 = 2 * NJ * G
            cons = {}

            for k in range(KT + 1):
                kp = k % 2
                if k < KT:
                    for h in range(2):
                        bank = kp * 2 + h
                        if (kp, h) in sc_free:
                            P.wait("tensor", "AC", sc_free[(kp, h)])
                        t_sc = _mm(P, psum[bank][:, 0:TQG],
                                   K_hp[h * 64:(h + 1) * 64,
                                        k * 128:(k + 1) * 128],
                                   Q_hp[h * 64:(h + 1) * 64,
                                        G * TQG:(G + 1) * TQG],
                                   True, True, tp=(h * 64, 0))
                if k >= 1:
                    kk = k - 1
                    kkp = kk % 2
                    sem, tv = cons.pop(kk)
                    P.wait("tensor", sem, tv)
                    if kk == 0:
                        P.wait("tensor", "DV", ctx_free[0])
                        P.wait("tensor", "DV", ctx_free[1])
                    for h in range(2):
                        cb = psum[4 + h]
                        for jj in range(NJ):
                            _mm(P, cb[:, jj * 65:jj * 65 + 65],
                                Pt[:, (kkp * 2 + h) * TQG + jj * 128:
                                   (kkp * 2 + h) * TQG + (jj + 1) * 128],
                                V_hp[:, kk * 130 + h * 65:kk * 130 + (h + 1) * 65],
                                kk == 0 and jj == 0,
                                kk == KT - 1 and jj == NJ - 1, skip=True)
                    pt_free[kkp] = P.cur("PE")
                if k < KT:
                    t_sc = P.mark("tensor", "PE")
                    P.wait("scalar", "PE", t_sc)
                    if kp in pt_free:
                        P.wait("scalar", "PE", pt_free[kp])
                    for h in range(2):
                        bank = kp * 2 + h
                        dst = Pt[:, (kp * 2 + h) * TQG:(kp * 2 + h + 1) * TQG]
                        tac = P.inc("scalar", lambda e, d=dst,
                                    s=psum[bank][:, 0:TQG]: e.activation(
                                    d, s, ACTF.Exp, scale=0.125), "AC")
                        sc_free[(kp, h)] = tac
                    if k >= m0:
                        mi = k - m0
                        P.wait("vector", "AC", tac)
                        if k == m0:
                            P.wait("vector", "IN", tconst)
                        for h in range(2):
                            pa = Pt[:, (kp * 2 + h) * TQG:(kp * 2 + h + 1) * TQG]
                            P.do("vector", lambda e, p=pa,
                                 m=masks[:, mi * TQG:(mi + 1) * TQG]:
                                 e.tensor_tensor(p, p, m, ALU.mult))
                        cons[k] = ("DV", P.mark("vector", "DV"))
                    else:
                        cons[k] = ("AC", tac)
            t_ctx = P.mark("tensor", "PE")
            P.wait("vector", "PE", t_ctx)
            for h in range(2):
                cb = psum[4 + h]
                hg = hp * 2 + h
                for jj in range(NJ):
                    J = G * NJ + jj
                    P.do("vector", lambda e, s=cb[:, jj * 65 + 64:jj * 65 + 65]:
                         e.reciprocal(rden[:, 0:1], s))
                    dst = A_q[:, J * D + hg * 64:J * D + hg * 64 + 64]
                    P.dve(lambda e, d=dst,
                          s=cb[:, jj * 65:jj * 65 + 64]:
                          e.tensor_scalar_mul(d, s, rden[:, 0:1]), dep=True)
                ctx_free[h] = P.mark("vector", "DV")
        kqv_pe = P.cur("PE")

    P.barrier()

    if stop == "P2":
        return _finish_dummy(cfg, P, H)
    carve.off = 0
    ctx_fm = carve.bf16(2 * DCH * 128)
    wo_sb = carve.bf16(DCH * D)
    x_st3 = carve.f32(2 * D)

    two = 0
    for dc in range(DCH):
        two = P.inc("gpsimd", lambda e,
                    d=wo_sb[:, dc * D:(dc + 1) * D],
                    s=H["wo_h"][dc * 128:(dc + 1) * 128, :]:
                    e.dma_start(out=d, in_=s), "WD", 16)
    wo_pe = [0, 0]
    x3_dv = [0, 0]
    res_tkt = {}
    for J in range(NQ):
        par = J % 2
        P.wait("sync", "DV", x3_dv[par])
        tin = _dma(P, "sync", "IN", x_st3[:, par * D:(par + 1) * D],
                   H["x_h"][2 * J * 128:(2 * J + 1) * 128, :])
        P.wait("vector", "PE", wo_pe[par])
        transpose_chunks(
            A_q[:, J * D:(J + 1) * D], DCH,
            lambda c, par=par: ctx_fm[:, par * DCH * 128 + c * 128:
                                      par * DCH * 128 + (c + 1) * 128])
        t_cfm = P.cur("DV")
        P.wait("tensor", "DV", t_cfm)
        if J == 0:
            P.wait("tensor", "WD", two)
        for do2 in range(D // 512):
            bank = par * 2 + do2
            if bank in res_tkt:
                P.wait("tensor", "DV", res_tkt[bank])
            for dc in range(DCH):
                tpe = _mm(P, psum[bank][:, 0:512],
                          ctx_fm[:, par * DCH * 128 + dc * 128:
                                 par * DCH * 128 + (dc + 1) * 128],
                          wo_sb[:, dc * D + do2 * 512:dc * D + do2 * 512 + 512],
                          dc == 0, dc == DCH - 1)
            P.wait("vector", "PE", tpe)
            P.wait("vector", "IN", tin)
            dst = A_own[:, J * D + do2 * 512:J * D + do2 * 512 + 512]
            P.do("vector", lambda e, d=dst, s=psum[bank][:, 0:512],
                 x_=x_st3[:, par * D + do2 * 512:par * D + do2 * 512 + 512]:
                 e.tensor_tensor(d, s, x_, ALU.add))
            res_tkt[bank] = P.mark("vector", "DV")
        x3_dv[par] = P.cur("DV")
        wo_pe[par] = P.cur("PE")

    P.barrier()

    if stop == "P3":
        return _finish_dummy(cfg, P, H)
    carve.off = 0
    import math
    WCH = math.gcd(D, DI)
    wdraw = carve.f32(DI)
    bmxA4 = carve.f32(WCH // 2)
    bmxB4 = carve.f32(max(WCH // 4, 16))
    bs_s4 = carve.f32(WCH // 32)
    bs_i4 = carve.f32(WCH // 32)
    wdq_d = carve.bf16(DI)
    p4a_end = carve.off

    raw_free = 0
    for oc in range(DCH):
        P.wait("sync", "DV", raw_free)
        tin = _dma(P, "sync", "W0", wdraw, H["wd_h"][oc * 128:(oc + 1) * 128, :])
        for sub in range(DI // WCH):
            tdq = dequant(wdraw[:, sub * WCH:(sub + 1) * WCH], WCH // 32,
                          wdq_d[:, sub * WCH:(sub + 1) * WCH], tin,
                          bmxA4, bmxB4, bs_s4, bs_i4, "W0")
        raw_free = tdq
        P.wait("tensor", "DV", tdq)
        transpose_chunks(
            wdq_d, ICH,
            lambda ic, oc=oc: A_xn[:, ic * D + oc * 128:ic * D + (oc + 1) * 128])
    wdT_done = P.cur("DV")

    if stop == "P4a":
        return _finish_dummy(cfg, P, H)
    P.wait("gpsimd", "DV", ffn_mx_tkt)
    tcc = _collective(P, H, 1, mxu, mxg)
    P.wait("vector", "GP", tcc)
    ops23 = scale_ops(0, 0) + scale_ops(8, 2)
    t_scb23 = _bcast_scalars(P, H, ops23, 4, scb[:, 2:6], tp_free)
    su_b, invu_b = scb[:, 2:3], scb[:, 3:4]
    sg_b, invg_b = scb[:, 4:5], scb[:, 5:6]

    pe_wdT = P.cur("PE")
    TGW = min(2 * TG, T)
    NTG2 = T // TGW
    carve.off = 0
    rb = carve.f32(2 * TGW)
    gq_b = carve.f32(TGW)
    t2_b = carve.f32(TGW)
    G_b = carve.f32(TGW)
    g2_b = carve.f32(2 * TGW)
    p4b_end = carve.off

    P.wait("vector", "DV", t_scb23)
    P.wait("sync", "DV", raw_free)
    P.wait("sync", "PE", pe_wdT)
    P.wait("sync", "SP", P.cur("SP"))
    rb_dv = {}
    sp_tkt = {}
    first = True
    for ic in range(ICH):
        for tg_i in range(NTG2):
            par = (ic * NTG2 + tg_i) % 2
            slot = rb[:, par * TGW:(par + 1) * TGW]
            if par in rb_dv:
                P.wait("sync", "DV", rb_dv[par])
            tin = _dma(P, "sync", f"R{par}", slot,
                       H["sp_g"][ic * 128:(ic + 1) * 128,
                                 tg_i * TGW:(tg_i + 1) * TGW])
            P.wait("vector", f"R{par}", tin)
            P.dve(lambda e, s=slot: e.tensor_scalar(
                gq_b, s, invg_b, MAGIC, ALU.mult, ALU.add), dep=True)
            P.dve(lambda e: e.tensor_scalar(
                gq_b, gq_b, MAGIC, sg_b, ALU.subtract, ALU.mult), dep=True)
            tq = P.mark("vector", "DV")
            rb_dv[par] = tq
            P.wait("scalar", "DV", tq)
            tsg = P.inc("scalar", lambda e: e.activation(
                t2_b, gq_b, ACTF.Sigmoid), "AC")
            P.wait("vector", "AC", tsg)
            P.dve(lambda e: e.tensor_scalar(
                t2_b, t2_b, 65536.0, MAGIC, ALU.mult, ALU.add), dep=True)
            P.dve(lambda e: e.tensor_scalar(
                G_b, t2_b, MAGIC, 65535.0, ALU.subtract, ALU.min), dep=True)
            g2slot = g2_b[:, par * TGW:(par + 1) * TGW]
            if par in sp_tkt:
                P.wait("vector", "SP", sp_tkt[par])
            P.dve(lambda e, d=g2slot: e.tensor_tensor(
                d, G_b, gq_b, ALU.mult), dep=True)
            P.dve(lambda e, d=g2slot: e.reduce_max(
                tmpm, d, AX.X, apply_absolute_value=True), dep=True)
            if first:
                P.dve(lambda e: e.tensor_copy(mx2, tmpm), dep=True)
                first = False
            else:
                P.dve(lambda e: e.tensor_tensor(mx2, mx2, tmpm, ALU.max),
                      dep=True)
            tsp = P.mark("vector", "DV")
            P.wait("scalar", "DV", tsp)
            sp_tkt[par] = _dma(P, "scalar", "SP",
                               H["sp_g2"][ic * 128:(ic + 1) * 128,
                                          tg_i * TGW:(tg_i + 1) * TGW], g2slot)
            if getattr(cfg, "s23_serial", False):
                P.barrier(flush_dma=True)
    t_mx2 = P.cur("DV")

    P.wait("gpsimd", "DV", t_mx2)
    tcc4 = _collective(P, H, 2, mx2)
    P.wait("vector", "GP", tcc4)
    ops4 = [lambda e: e.tensor_scalar_mul(rowb[0:1, 0:1], rowb[0:1, 0:1],
                                          1.0 / 65536.0),
            lambda e: e.drain()]
    ops4 += scale_ops(0, 0)
    ops4 += [lambda e: e.tensor_scalar_mul(rowb[0:1, 1:2], rowb[0:1, 1:2],
                                           1.0 / 65536.0),
             lambda e: e.drain()]
    t_scb4 = _bcast_scalars(P, H, ops4, 2, scb[:, 6:8], tp_free)
    s2_b, inv2_b = scb[:, 6:7], scb[:, 7:8]

    if stop == "S2":
        return _finish_dummy(cfg, P, H)
    carve.off = 2 * TGW
    rbu = carve.f32(2 * TGW)
    uq_b = carve.f32(TGW)
    op_b = carve.f32(2 * TGW)
    p4c_end = carve.off
    assert p4c_end <= 8064

    P.wait("vector", "DV", t_scb4)
    P.wait("sync", "SP", P.cur("SP"))
    for pkey in list(sp_tkt):
        P.wait("vector", "SP", sp_tkt[pkey])
    spo_tkt = {}
    first = True
    for ic in range(ICH):
        for tg_i in range(NTG2):
            par = (ic * NTG2 + tg_i) % 2
            slot_g2 = rb[:, par * TGW:(par + 1) * TGW]
            slot_u = rbu[:, par * TGW:(par + 1) * TGW]
            if par in rb_dv:
                P.wait("sync", "DV", rb_dv[par])
            tg2 = _dma(P, "sync", f"R{par}", slot_g2,
                       H["sp_g2"][ic * 128:(ic + 1) * 128,
                                  tg_i * TGW:(tg_i + 1) * TGW])
            tin = _dma(P, "sync", f"R{par + 2}", slot_u,
                       H["sp_up"][ic * 128:(ic + 1) * 128,
                                  tg_i * TGW:(tg_i + 1) * TGW])
            P.wait("vector", f"R{par}", tg2)
            P.wait("vector", f"R{par + 2}", tin)
            P.dve(lambda e, s=slot_u: e.tensor_scalar(
                uq_b, s, invu_b, MAGIC, ALU.mult, ALU.add), dep=True)
            P.dve(lambda e: e.tensor_scalar(
                uq_b, uq_b, MAGIC, su_b, ALU.subtract, ALU.mult), dep=True)
            P.dve(lambda e, s=slot_g2: e.tensor_scalar(
                s, s, inv2_b, MAGIC, ALU.mult, ALU.add), dep=True)
            P.dve(lambda e, s=slot_g2: e.tensor_scalar(
                s, s, MAGIC, s2_b, ALU.subtract, ALU.mult), dep=True)
            opslot = op_b[:, par * TGW:(par + 1) * TGW]
            if par in spo_tkt:
                P.wait("vector", "SP", spo_tkt[par])
            P.dve(lambda e, d=opslot, s=slot_g2: e.tensor_tensor(
                d, s, uq_b, ALU.mult), dep=True)
            P.dve(lambda e, d=opslot: e.reduce_max(
                tmpm, d, AX.X, apply_absolute_value=True), dep=True)
            if first:
                P.dve(lambda e: e.tensor_copy(mxo, tmpm), dep=True)
                first = False
            else:
                P.dve(lambda e: e.tensor_tensor(mxo, mxo, tmpm, ALU.max),
                      dep=True)
            tsp = P.mark("vector", "DV")
            rb_dv[par] = tsp
            P.wait("scalar", "DV", tsp)
            spo_tkt[par] = _dma(P, "scalar", "SP",
                                H["sp_o"][ic * 128:(ic + 1) * 128,
                                          tg_i * TGW:(tg_i + 1) * TGW], opslot)
            if getattr(cfg, "s23_serial", False):
                P.barrier(flush_dma=True)
    t_mxo = P.cur("DV")

    P.wait("gpsimd", "DV", t_mxo)
    tcc5 = _collective(P, H, 3, mxo)
    P.wait("vector", "GP", tcc5)
    t_scb5 = _bcast_scalars(P, H, scale_ops(0, 0), 2, scb[:, 8:10], tp_free)
    so_b, invo_b = scb[:, 8:9], scb[:, 9:10]

    P.barrier()

    if stop == "S3":
        P.wait("sync", "SP", P.cur("SP"))
        tpd = 0
        slot_i = 0
        for ic_d in (0, 5, 7, 11, 12, 17, 21, 23):
            for tg_d in (0, 2):
                tpd = _dma(P, "sync", "IN",
                           H["A_own"][:, D + slot_i * TG:D + (slot_i + 1) * TG],
                           H["sp_o"][ic_d * 128:(ic_d + 1) * 128,
                                     tg_d * TG:(tg_d + 1) * TG])
                slot_i += 1
        P.wait("sync", "IN", tpd)
        return _finish_dummy(cfg, P, H)
    carve.off = 0
    rbo = carve.f32(2 * TG)
    oq_scr = carve.f32(TG)

    P.wait("vector", "DV", t_scb5)
    P.wait("sync", "DV", t_mxo)
    P.wait("sync", "SP", P.cur("SP"))
    rbo_dv = {}
    dbank = {}
    tout = 0
    oar_pe = 0
    for tg_i in range(NTG):
        for ic in range(ICH):
            par = (tg_i * ICH + ic) % 2
            slot = rbo[:, par * TG:(par + 1) * TG]
            if par in rbo_dv:
                P.wait("sync", "DV", rbo_dv[par])
            tin = _dma(P, "sync", f"R{par}", slot,
                       H["sp_o"][ic * 128:(ic + 1) * 128,
                                 tg_i * TG:(tg_i + 1) * TG])
            P.wait("vector", f"R{par}", tin)
            if ic == 0:
                P.wait("vector", "PE", oar_pe)
            P.dve(lambda e, s=slot: e.tensor_scalar(
                oq_scr, s, invo_b, MAGIC, ALU.mult, ALU.add), dep=True)
            dst = A_q[:, ic * TG:(ic + 1) * TG]
            P.dve(lambda e, d=dst: e.tensor_scalar(
                d, oq_scr, MAGIC, so_b, ALU.subtract, ALU.mult), dep=True)
            rbo_dv[par] = P.mark("vector", "DV")
        t_oar = P.cur("DV")
        if getattr(cfg, "p5_serial", False):
            P.barrier(flush_dma=True)
        P.wait("tensor", "DV", t_oar)
        if tg_i == 0:
            P.wait("tensor", "DV", wdT_done)
        for tt_ in range(TG // 128):
            J = tg_i * (TG // 128) + tt_
            for do2 in range(D // 512):
                bank = (tt_ % 2) * 2 + do2
                if bank in dbank:
                    P.wait("tensor", "DV", dbank[bank])
                for ic in range(ICH):
                    tpe = _mm(P, psum[bank][:, 0:512],
                              A_q[:, ic * TG + tt_ * 128:ic * TG + tt_ * 128 + 128],
                              A_xn[:, ic * D + do2 * 512:ic * D + do2 * 512 + 512],
                              ic == 0, ic == ICH - 1)
                tpe = P.mark("tensor", "PE")
                P.wait("vector", "PE", tpe)
                dst = A_own[:, J * D + do2 * 512:J * D + do2 * 512 + 512]
                if getattr(cfg, "ffn_only", False):
                    P.do("vector", lambda e, d=dst, s=psum[bank][:, 0:512]:
                         e.tensor_copy(d, s))
                else:
                    P.do("vector", lambda e, d=dst, s=psum[bank][:, 0:512]:
                         e.tensor_tensor(d, d, s, ALU.add))
                dbank[bank] = P.mark("vector", "DV")
            for do2 in range(D // 512):
                P.wait("sync", "DV", dbank[(tt_ % 2) * 2 + do2])
            tout = _dma(P, "sync", "OUT", H["out_h"][J * 128:(J + 1) * 128, :],
                        A_own[:, J * D:(J + 1) * D])
        oar_pe = P.cur("PE")
        if getattr(cfg, "p5_serial", False):
            P.barrier(flush_dma=True)

    P.wait("sync", "OUT", tout)
    P.wait("gpsimd", "OUT", tout)


def _collective(P, H, rnd, mx_a, mx_b=None):
    rowb = H["row"]
    cin, cout = H["ccio"][rnd]
    ncores = H["cfg"].ncores
    P.do("gpsimd", lambda e, a=mx_a: e.tensor_reduce(
        rowb[0:1, 0:1], a, AX.C, ALU.max))
    if mx_b is not None:
        P.do("gpsimd", lambda e, a=mx_b: e.tensor_reduce(
            rowb[0:1, 8:9], a, AX.C, ALU.max))
    tg0 = P.mark("gpsimd", "GP")
    P.wait("vector", "GP", tg0)
    P.do("vector", lambda e: e.tensor_copy(
        rowb[0:1, 1:8], rowb[0:1, 0:1].broadcast_to((1, 7))))
    if mx_b is not None:
        P.do("vector", lambda e: e.tensor_copy(
            rowb[0:1, 9:16], rowb[0:1, 8:9].broadcast_to((1, 7))))
    else:
        P.do("vector", lambda e: e.tensor_copy(
            rowb[0:1, 8:16], rowb[0:1, 0:1].broadcast_to((1, 8))))
    tv0 = P.mark("vector", "DV")
    P.wait("gpsimd", "DV", tv0)
    t = _dma(P, "gpsimd", "WD", cin[:], rowb[0:1, 0:16])
    P.wait("gpsimd", "WD", t)
    tcc = P.inc("gpsimd",
                lambda e, ci=cin, co=cout, n=ncores: e.collective_compute(
                    "AllReduce", ALU.max, replica_groups=[list(range(n))],
                    ins=[ci[:]], outs=[co[:]]),
                "CC")
    P.wait("gpsimd", "CC", tcc)
    t2 = _dma(P, "gpsimd", "WD", rowb[0:1, 0:16], cout[:].unsqueeze(0))
    P.wait("gpsimd", "WD", t2)
    return P.mark("gpsimd", "GP")


def _bcast_scalars(P, H, dve_ops, n, scb_dst, tp_free):
    rowb, ones, pbank = H["row"], H["ones"], H["psum"][7]
    for f in dve_ops:
        P.do("vector", f)
    t = P.mark("vector", "DV")
    P.wait("tensor", "DV", t)
    P.wait("tensor", "DV", tp_free[1])
    _mm(P, pbank[:, 0:n], ones[:, :], rowb[0:1, 0:n], True, True)
    tp = P.mark("tensor", "PE")
    P.wait("vector", "PE", tp)
    P.do("vector", lambda e, d=scb_dst, s=pbank[:, 0:n]: e.tensor_copy(d, s))
    tkt = P.mark("vector", "DV")
    tp_free[1] = tkt
    return tkt



def make_masks(cfg, h):
    NJ, TQG = cfg.NJ, cfg.TQG
    tril = np.triu(np.ones((128, 128), np.float32))
    m = np.zeros((2 * NJ * 128, TQG), dtype=np.float32)
    for mi in range(2 * NJ):
        for jj in range(NJ):
            blk = m[mi * 128:(mi + 1) * 128, jj * 128:(jj + 1) * 128]
            thr = 2 * jj
            if mi < thr:
                blk[:] = 1.0
            elif mi == thr:
                blk[:] = tril
            elif mi == thr + 1 and h == 1:
                blk[:] = 1.0
    return m.astype(ml_dtypes.bfloat16)


def permute_x(x_b, h):
    if h == 0:
        return x_b
    S, D = x_b.shape
    return np.ascontiguousarray(
        x_b.reshape(S // 256, 2, 128, D)[:, ::-1].reshape(S, D))


def kernel(L0_attn_in_features, L0_attn_Wq, L0_attn_Wk, L0_attn_Wv, L0_attn_Wo,
           L0_ffn_w_gate, L0_ffn_w_up, L0_ffn_w_down):
    cfg = Config(4, 4096, 1024, 16, 3072, 8)
    f = lambda a: np.ascontiguousarray(np.asarray(a, dtype=np.float32))
    x = f(L0_attn_in_features)
    consts = dict(wq=f(L0_attn_Wq), wk=f(L0_attn_Wk), wv=f(L0_attn_Wv),
                  wo=f(L0_attn_Wo), wup=f(L0_ffn_w_up), wgate=f(L0_ffn_w_gate),
                  wdown=f(L0_ffn_w_down),
                  ident=np.eye(128, dtype=ml_dtypes.bfloat16),
                  ones=np.ones((1, 128), np.float32))
    masks = [make_masks(cfg, h) for h in (0, 1)]
    in_maps = []
    for c in range(cfg.ncores):
        b, h = c // 2, c % 2
        m = dict(consts)
        m["x"] = permute_x(x[b], h)
        m["masks"] = masks[h]
        in_maps.append(m)
    nc = build_nc(cfg)
    res = run_bass_kernel_spmd(nc, in_maps, core_ids=list(range(cfg.ncores)))
    out = np.empty_like(x)
    for c in range(cfg.ncores):
        b, h = c // 2, c % 2
        r = np.asarray(res.results[c]["out"])
        for j in range(cfg.NQ):
            g = (2 * j + h) * 128
            out[b, g:g + 128] = r[j * 128:(j + 1) * 128]
    return out
